# version 50
# baseline (speedup 1.0000x reference)
import numpy as np
import ml_dtypes
import concourse.bass as bass
import concourse.mybir as mybir
from concourse.bass_utils import run_bass_kernel_spmd
from contextlib import ExitStack

F32 = mybir.dt.float32
BF16 = mybir.dt.bfloat16
I32 = mybir.dt.int32
U8 = mybir.dt.uint8
AF = mybir.ActivationFunctionType
ALU = mybir.AluOpType
AX = mybir.AxisListType

H = 2048
NH = H // 128
I_FFN = 8192
NI = I_FFN // 128
DK = 128
EPS = 1e-5
BIGMASK = 3e38


class Buf:
    __slots__ = ("name", "last_w", "readers", "excl")

    def __init__(self, name, excl=False):
        self.name = name
        self.last_w = None
        self.readers = []
        self.excl = excl


class Em:

    NQ = 8

    def __init__(self, nc, ctx):
        self.nc = nc
        names = ["pe", "act", "dve", "pool"]
        self.sem = {n: ctx.enter_context(nc.semaphore("sem_" + n)) for n in names}
        self.cnt = {n: 0 for n in names}
        for q in ("dS", "dA", "dP", "dV"):
            for i in range(self.NQ):
                n = f"{q}{i}"
                self.sem[n] = ctx.enter_context(nc.semaphore("sem_" + n))
                self.cnt[n] = 0
        self.qissue = {"dS": 0, "dA": 0, "dP": 0, "dV": 0}
        self.eng = {"pe": nc.tensor, "act": nc.scalar, "dve": nc.vector,
                    "pool": nc.gpsimd, "sync": nc.sync}
        self.qeng = {"dS": "sync", "dA": "act", "dP": "pool",
                     "dV": "pool"}
        self.waited = {n: {} for n in self.eng}

    def _wait(self, ename, sem_name, val):
        if val <= 0:
            return
        w = self.waited[ename]
        if w.get(sem_name, 0) >= val:
            return
        self.eng[ename].wait_ge(self.sem[sem_name], val)
        w[sem_name] = val

    def dep_wait(self, ename, reads, writes):
        need = {}

        def add(tag):
            if tag is not None and need.get(tag[0], 0) < tag[1]:
                need[tag[0]] = tag[1]
        for b in reads:
            add(b.last_w)
            if b.excl:
                for t in b.readers:
                    add(t)
        for b in writes:
            add(b.last_w)
            for t in b.readers:
                add(t)
        for s, v in need.items():
            self._wait(ename, s, v)

    def _finish(self, ins, sem_name, inc, reads, writes):
        ins.then_inc(self.sem[sem_name], inc)
        self.cnt[sem_name] += inc
        tag = (sem_name, self.cnt[sem_name])
        for b in writes:
            b.last_w = tag
            b.readers = []
        for b in reads:
            b.readers.append(tag)
        return ins

    def op(self, ename, fn, reads=(), writes=()):
        reads = list(reads)
        writes = list(writes)
        self.dep_wait(ename, reads, writes)
        ins = fn(self.eng[ename])
        return self._finish(ins, ename, 1, reads, writes)

    def dma(self, q, out_ap, in_ap, reads=(), writes=(), transpose=False):
        reads = list(reads)
        writes = list(writes)
        ename = self.qeng[q]
        j = self.qissue[q]
        self.qissue[q] += 1
        sname = f"{q}{j % self.NQ}"
        self._wait(ename, sname, self.cnt[sname])
        self.dep_wait(ename, reads, writes)
        if transpose:
            ins = self.eng[ename].dma_start_transpose(out_ap, in_ap)
        else:
            ins = self.eng[ename].dma_start(out_ap, in_ap)
        return self._finish(ins, sname, 16, reads, writes)

    def mm_plan(self, plan, reads=()):
        reads = list(reads)
        self.dep_wait("pe", reads, [])
        allg = []
        for entry in plan:
            buf, mms = entry[0], entry[1]
            greads = list(entry[2]) if len(entry) > 2 else []
            allg.append(greads)
            self.dep_wait("pe", greads, [buf])
            for k, (out_ap, l, r) in enumerate(mms):
                last = k == len(mms) - 1
                ins = self.nc.tensor.matmul(out_ap, l, r, start=(k == 0), stop=last)
                if last:
                    self._finish(ins, "pe", 1, [], [buf])
        tag = ("pe", self.cnt["pe"])
        for b in reads:
            b.readers.append(tag)
        for greads in allg:
            for b in greads:
                b.readers.append(tag)

    def barrier(self):
        for ename in self.eng:
            for s, v in self.cnt.items():
                self._wait(ename, s, v)

    def drain_dmas(self):
        for q in ("dS", "dA", "dP", "dV"):
            for i in range(self.NQ):
                n = f"{q}{i}"
                self._wait("sync", n, self.cnt[n])


def tgroups(t):
    out = []
    o = 0
    while o < t:
        w = min(512, t - o)
        out.append((o, w))
        o += w
    return out


_uid = [0]


def uname(base):
    _uid[0] += 1
    return f"{base}_{_uid[0]}"


def build_program(T_OWN, T_PRE, stop=99, scan_stop=99):
    TT = T_OWN + T_PRE
    nT = TT // 128
    nTo = T_OWN // 128
    nPre = T_PRE // 128
    nc = bass.Bass("TRN2", target_bir_lowering=False)

    dx_own = nc.dram_tensor("x_own", [T_OWN, H], F32, kind="ExternalInput").ap()
    dx_pre = nc.dram_tensor("x_pre", [T_PRE, H], F32, kind="ExternalInput").ap()
    dwg4 = nc.dram_tensor("wg4", [NI, 128, H], U8, kind="ExternalInput").ap()
    dwu4 = nc.dram_tensor("wu4", [NI, 128, H], U8, kind="ExternalInput").ap()
    dwd4 = nc.dram_tensor("wd4", [2, NI, 128, H // 2], U8,
                          kind="ExternalInput").ap()
    dgs = nc.dram_tensor("gs2", [128, NI], F32, kind="ExternalInput").ap()
    dus = nc.dram_tensor("us2", [128, NI], F32, kind="ExternalInput").ap()
    ddsb = nc.dram_tensor("dsb", [128, H], F32, kind="ExternalInput").ap()
    dgnw = nc.dram_tensor("gnw2", [128, NH], F32, kind="ExternalInput").ap()
    dwq = nc.dram_tensor("wq2", [H, DK], F32, kind="ExternalInput").ap()
    dwk = nc.dram_tensor("wk2", [H, DK], F32, kind="ExternalInput").ap()
    dwog = nc.dram_tensor("wog4", [NH, NH, 128, 128], F32, kind="ExternalInput").ap()
    dwig = nc.dram_tensor("wig4", [NH, NH, 128, 128], F32, kind="ExternalInput").ap()
    dwo = nc.dram_tensor("wo3", [NH, 128, H], F32, kind="ExternalInput").ap()
    dident = nc.dram_tensor("ident", [128, 128], BF16, kind="ExternalInput").ap()
    dumask = nc.dram_tensor("umask", [128, 128], F32, kind="ExternalInput").ap()
    dcsts = nc.dram_tensor("csts", [128, 2], F32, kind="ExternalInput").ap()

    dout = nc.dram_tensor("out", [T_OWN, H], F32, kind="ExternalOutput").ap()

    dx1 = nc.dram_tensor("x1s", [TT, H], F32, kind="Internal").ap()
    dx2 = nc.dram_tensor("x2s", [T_OWN, H], F32, kind="Internal").ap()
    dh4 = nc.dram_tensor("h4s", [nT, NI, 128, 128], BF16, kind="Internal").ap()
    dgw = nc.dram_tensor("gwig", [T_OWN // 512, NH, 128, 512], BF16,
                         kind="Internal").ap()
    dxn2 = nc.dram_tensor("xn2tok", [nT, 128, H], BF16, kind="Internal").ap()
    dwbfa = nc.dram_tensor("wbfa", [3, NH, 128, H], BF16, kind="Internal").ap()

    with ExitStack() as ctx:
        em = Em(nc, ctx)
        AL = ctx.enter_context

        s_ident = AL(nc.sbuf_tensor("s_ident", [128, 128], BF16))
        s_umask = AL(nc.sbuf_tensor("s_umask", [128, 128], F32))
        s_gs = AL(nc.sbuf_tensor("s_gs", [128, NI], F32))
        s_us = AL(nc.sbuf_tensor("s_us", [128, NI], F32))
        s_gnw = AL(nc.sbuf_tensor("s_gnw", [128, NH], F32))
        s_ones1 = AL(nc.sbuf_tensor("s_ones1", [128, 1], BF16))
        s_onesr = AL(nc.sbuf_tensor("s_onesr", [1, 128], F32))
        s_cst = AL(nc.sbuf_tensor("s_cst", [128, 2], F32))
        b_c = Buf("consts")
        for dst, src in [(s_ident, dident), (s_umask, dumask), (s_gs, dgs),
                         (s_us, dus), (s_gnw, dgnw), (s_cst, dcsts)]:
            em.dma("dS", dst[:, :], src, writes=[b_c])
        em.op("dve", lambda v: v.memset(s_ones1[:, :], 1.0), writes=[b_c])
        em.op("dve", lambda v: v.memset(s_onesr[:, :], 1.0), writes=[b_c])

        def ln_transpose(al, srcs, xnT, bxnT, TT_loc, tok_dst=None,
                         nxn=3):
            if True:
                a2 = al
                pT = [a2(nc.psum_tensor(uname("pT"), [128, 1024], BF16))
                      for _ in range(2)]
                bpT = [Buf(uname("bpT"), excl=True) for _ in range(2)]
                s_x = [a2(nc.sbuf_tensor(uname("lnx"), [128, H], F32))
                       for _ in range(2)]
                s_xn = [a2(nc.sbuf_tensor(uname("lnxn"), [128, H], BF16))
                        for _ in range(nxn)]
                s_st = a2(nc.sbuf_tensor(uname("lnst"), [128, 24], F32))
                bx = [Buf("lnx0"), Buf("lnx1")]
                bxn = [Buf(uname("lnxn")) for _ in range(3)]
                bstl = [Buf(uname("lnst")) for _ in range(3)]
                bsql = [Buf(uname("lnsq")) for _ in range(3)]
                xnT3 = xnT[:, :].rearrange("p (h t) -> p h t", t=TT_loc)
                tiles = []
                for (src, row0, ntiles) in srcs:
                    for kk in range(ntiles):
                        tiles.append(src[row0 + kk * 128: row0 + (kk + 1) * 128, :])

                def xload(ti):
                    em.dma("dA", s_x[ti % 2][:, :], tiles[ti],
                           writes=[bx[ti % 2]])

                def head(tj):
                    slh = tj % nxn
                    sth = s_st[:, slh * 8:slh * 8 + 8]
                    Xh, BXh = s_x[tj % 2], bx[tj % 2]
                    XNh, BXNh = s_xn[slh], bxn[slh]
                    em.op("dve", lambda v: v.reduce_sum(
                        sth[:, 0:1], Xh[:, :], AX.X), reads=[BXh],
                        writes=[bstl[slh]])
                    em.op("act", lambda a: a.activation(
                        XNh[:, :], Xh[:, :], AF.Square,
                        accum_out=sth[:, 1:2]),
                        reads=[BXh], writes=[BXNh, bsql[slh]])

                def _gen():
                    xload(0)
                    if len(tiles) > 1:
                        xload(1)
                    head(0)
                    for ti in range(len(tiles)):
                        sl = ti % nxn
                        c0 = sl * 8
                        X, BX = s_x[ti % 2], bx[ti % 2]
                        XN, BXN = s_xn[sl], bxn[sl]
                        bst = bstl[sl]
                        st = s_st[:, c0:c0 + 8]
                        bsq = bsql[sl]
                        if ti + 1 < len(tiles):
                            head(ti + 1)
                        em.op("dve", lambda v: v.tensor_scalar(
                            st[:, 2:3], st[:, 0:1], 1.0 / H, None, ALU.mult),
                            reads=[bst], writes=[bst])
                        em.op("dve", lambda v: v.tensor_tensor(
                            st[:, 3:4], st[:, 2:3], st[:, 2:3], ALU.mult),
                            reads=[bst], writes=[bst])
                        em.op("dve", lambda v: v.scalar_tensor_tensor(
                            st[:, 4:5], st[:, 1:2], 1.0 / H, st[:, 3:4],
                            ALU.mult, ALU.subtract), reads=[bst, bsq],
                            writes=[bst])
                        em.op("act", lambda a: a.activation(
                            st[:, 5:6], st[:, 4:5], AF.Ln,
                            bias=s_cst[:, 0:1]),
                            reads=[bst, b_c], writes=[bst])
                        em.op("act", lambda a: a.activation(
                            st[:, 6:7], st[:, 5:6], AF.Exp, scale=-0.5),
                            reads=[bst], writes=[bst])
                        em.op("dve", lambda v: v.scalar_tensor_tensor(
                            st[:, 7:8], st[:, 2:3], -1.0, st[:, 6:7],
                            ALU.mult, ALU.mult), reads=[bst], writes=[bst])
                        em.op("dve", lambda v: v.tensor_scalar(
                            XN[:, :], X[:, :], st[:, 6:7], st[:, 7:8],
                            ALU.mult, ALU.add), reads=[BX, bst], writes=[BXN])
                        if ti + 2 < len(tiles):
                            xload(ti + 2)
                        if tok_dst is not None:
                            em.dma("dV", tok_dst[ti], XN[:, :], reads=[BXN])
                        for half in range(2):
                            pb = bpT[half]
                            em.dep_wait("pe", [BXN, b_c], [pb])
                            for hh in range(8):
                                hc = half * 8 + hh
                                ins = nc.tensor.transpose(
                                    pT[half][:, hh * 128:(hh + 1) * 128],
                                    XN[:, hc * 128:(hc + 1) * 128], s_ident[:, :])
                                if hh == 7:
                                    em._finish(ins, "pe", 1, [BXN, b_c], [pb])
                            if half == 0:
                                em.op("act", lambda a, ti=ti: a.activation(
                                    xnT3[:, 0:8, ti * 128:(ti + 1) * 128],
                                    pT[0][:, :].rearrange(
                                        "p (h t) -> p h t", t=128),
                                    AF.Copy), reads=[pb], writes=[bxnT[ti]])
                            else:
                                em.op("dve", lambda v, ti=ti: v.tensor_copy(
                                    xnT3[:, 8:16, ti * 128:(ti + 1) * 128],
                                    pT[1][:, :].rearrange(
                                        "p (h t) -> p h t", t=128)),
                                    reads=[pb], writes=[bxnT[ti]])
                        yield ti
                return _gen()

        def drive(gen, done, need):
            if gen is None:
                return
            while done[0] < need:
                try:
                    next(gen)
                except StopIteration:
                    break
                done[0] += 1

        def drain(gen):
            if gen is not None:
                for _ in gen:
                    pass

        def ffn_gu(al, xnT, bxnT, TT_loc, first_call, ln_gen=None):
            tgs = tgroups(TT_loc)
            with ExitStack() as ps:
                a2 = ps.enter_context
                al = a2
                s_wraw = [a2(nc.sbuf_tensor(uname("wraw"), [128, H], U8))
                          for _ in range(2)]
                s_wbf = [a2(nc.sbuf_tensor(uname("wbf"), [128, H], BF16))
                         for _ in range(10)]
                s_silu = [a2(nc.sbuf_tensor(uname("silu"), [128, TT_loc], BF16))
                          for _ in range(5)]
                s_h = [a2(nc.sbuf_tensor(uname("hsb"), [128, TT_loc], BF16))
                       for _ in range(5)]
                b_wraw = [Buf(uname("wraw")) for _ in range(2)]
                b_wbf = [Buf(uname("wbf")) for _ in range(10)]
                b_silu = [Buf(uname("silu")) for _ in range(5)]
                b_h = [Buf(uname("hsb")) for _ in range(5)]
                pG = a2(nc.psum_tensor(uname("pG"), [128, 2048], F32))
                b_pG = [Buf(uname("pG"), excl=True) for _ in range(4)]
                pG2 = [a2(nc.psum_tensor(uname("pG2"), [128, 512], F32))
                       for _ in range(2)]
                b_pG2 = [Buf(uname("pG2"), excl=True) for _ in range(2)]

                if first_call:
                    s_af = [al(nc.sbuf_tensor(uname("caf"), [128, H], F32))] * 2
                    s_ab = [al(nc.sbuf_tensor(uname("cab"), [128, H], BF16))] * 2
                    b_af = [Buf(uname("caf"))] * 2
                    b_ab = [Buf(uname("cab"))] * 2
                    aconv = []
                    for wi2, wd4 in enumerate((dwog, dwig)):
                        for cb in range(NH):
                            aconv.append((wi2, cb,
                                          wd4[cb].rearrange("d p c -> p d c")))
                    for cb in range(NH):
                        aconv.append((2, cb, dwo[cb]))
                wsrc = {"g": (dwg4, s_gs), "u": (dwu4, s_us)}

                def loadconv(k):
                    ib, wi = k // 2, k % 2
                    raw = wsrc["g" if wi == 0 else "u"][0]
                    WB, BWB = s_wbf[k % 10], b_wbf[k % 10]
                    WR, BWR = s_wraw[k % 2], b_wraw[k % 2]
                    em.dma("dS", WR[:, :], raw[ib], writes=[BWR])
                    em.op("act", lambda a: a.activation(
                        WB[:, :], WR[:, :], AF.Copy, bias=-128.0),
                        reads=[BWR], writes=[BWB])

                def auxconv(ib):
                    sl = ib % 2
                    if ib < len(aconv):
                        wi2, cb, src_ap = aconv[ib]
                        if wi2 < 2:
                            em.dma("dS",
                                   s_af[sl][:, :].rearrange(
                                       "p (d c) -> p d c", d=NH),
                                   src_ap, writes=[b_af[sl]])
                        else:
                            em.dma("dS", s_af[sl][:, :], src_ap,
                                   writes=[b_af[sl]])
                        em.op("act", lambda a, sl=sl: a.activation(
                            s_ab[sl][:, :], s_af[sl][:, :], AF.Copy),
                            reads=[b_af[sl]], writes=[b_ab[sl]])
                        em.dma("dP", dwbfa[wi2, cb], s_ab[sl][:, :],
                               reads=[b_ab[sl]])

                WARM = 5
                for kw in range(2 * WARM):
                    loadconv(kw)

                def step_evac(ib, wi, o, wd, pap, pbuf):
                    scale2 = wsrc["g" if wi == 0 else "u"][1]
                    SL, BSL = s_silu[ib % 5], b_silu[ib % 5]
                    if wi == 0:
                        em.op("act", lambda a: a.activation(
                            SL[:, o:o + wd], pap, AF.Silu,
                            scale=scale2[:, ib:ib + 1]),
                            reads=[pbuf, b_c], writes=[BSL])
                    else:
                        HT, BHT = s_h[ib % 5], b_h[ib % 5]
                        em.op("dve", lambda v: v.scalar_tensor_tensor(
                            HT[:, o:o + wd], pap, scale2[:, ib:ib + 1],
                            SL[:, o:o + wd], ALU.mult, ALU.mult),
                            reads=[pbuf, BSL, b_c], writes=[BHT])

                lnd = [0]
                for (o, wd) in tgs:
                    drive(ln_gen, lnd, (o + wd + 127) // 128)
                    for kw in range(2 * WARM):
                        ibw, wiw = kw // 2, kw % 2
                        WB, BWB = s_wbf[kw % 10], b_wbf[kw % 10]
                        if o < 2048:
                            pap, pbuf = pG[:, o:o + wd], b_pG[o // 512]
                        else:
                            pap, pbuf = pG2[wiw][:, 0:wd], b_pG2[wiw]
                        mms = [(pap, WB[:, hc * 128:(hc + 1) * 128],
                                xnT[:, hc * TT_loc + o: hc * TT_loc + o + wd])
                               for hc in range(NH)]
                        em.mm_plan([(pbuf, mms,
                                     bxnT[o // 128:(o + wd + 127) // 128])],
                                   reads=[BWB])
                        step_evac(ibw, wiw, o, wd, pap, pbuf)
                for ibw in range(WARM):
                    HT, BHT = s_h[ibw % 5], b_h[ibw % 5]
                    em.dma("dV",
                           dh4[0:TT_loc // 128, ibw, :, :]
                           .rearrange("s p t -> p s t"),
                           HT[:, :].rearrange("p (s t) -> p s t", t=128),
                           reads=[BHT])
                drain(ln_gen)
                if 2 * WARM < 2 * NI:
                    loadconv(2 * WARM)

                for ib in range(WARM, NI):
                    for wi, w in enumerate(("g", "u")):
                        k = ib * 2 + wi
                        WB, BWB = s_wbf[k % 10], b_wbf[k % 10]
                        if 2 * WARM + 1 <= k + 1 < 2 * NI:
                            loadconv(k + 1)
                        def pslice(o, wdth):
                            if o < 2048:
                                return pG[:, o:o + wdth], b_pG[o // 512]
                            return pG2[wi][:, 0:wdth], b_pG2[wi]
                        plan = []
                        for (o, wd) in tgs:
                            pap, pbuf = pslice(o, wd)
                            mms = [(pap, WB[:, hc * 128:(hc + 1) * 128],
                                    xnT[:, hc * TT_loc + o: hc * TT_loc + o + wd])
                                   for hc in range(NH)]
                            plan.append((pbuf, mms,
                                         bxnT[o // 128:(o + wd + 127) // 128]))
                        em.mm_plan(plan, reads=[BWB])
                        if w == "g":
                            for (o, wd) in tgs:
                                pap, pbuf = pslice(o, wd)
                                step_evac(ib, 0, o, wd, pap, pbuf)
                        else:
                            for (o, wd) in tgs:
                                pap, pbuf = pslice(o, wd)
                                step_evac(ib, 1, o, wd, pap, pbuf)
                            HT, BHT = s_h[ib % 5], b_h[ib % 5]
                            em.dma("dV",
                                   dh4[0:TT_loc // 128, ib, :, :]
                                   .rearrange("s p t -> p s t"),
                                   HT[:, :].rearrange("p (s t) -> p s t", t=128),
                                   reads=[BHT])
                            if first_call:
                                auxconv(ib - WARM)
            em.barrier()

        def ffn_down(al, TT_loc, first_call, res_srcs, dst, scale_half):
            nT_loc = TT_loc // 128
            HH = H // 2
            with ExitStack() as ps:
                a2 = ps.enter_context
                s_wd = a2(nc.sbuf_tensor(uname("wdsb"), [128, NI * HH], BF16))
                b_wd = Buf(uname("wd"))
                s_hts = [a2(nc.sbuf_tensor(uname("hts"), [128, 2048], BF16))
                         for _ in range(3)]
                b_hts = [Buf(uname("hts")) for _ in range(3)]
                s_dsb = a2(nc.sbuf_tensor(uname("dsbl"), [128, H], F32))
                b_dsb = Buf(uname("dsb"))
                s_xr = [a2(nc.sbuf_tensor(uname("xr"), [128, HH], F32))
                        for _ in range(2)]
                b_xr = [Buf(uname("xr")) for _ in range(2)]
                s_t1 = [a2(nc.sbuf_tensor(uname("t1"), [128, HH], F32))
                        for _ in range(2)]
                b_t1 = [Buf(uname("t1")) for _ in range(2)]
                s_xo = [a2(nc.sbuf_tensor(uname("xo"), [128, HH], F32))
                        for _ in range(2)]
                b_xo = [Buf(uname("xo")) for _ in range(2)]
                pD = [a2(nc.psum_tensor(uname("pD"), [128, 1024], F32))
                      for _ in range(2)]
                b_pD = [Buf(uname("pD"), excl=True) for _ in range(2)]
                em.dma("dS", s_dsb[:, :], ddsb, writes=[b_dsb])

                rows = []
                for (src, row0, ntiles) in res_srcs:
                    for kk in range(ntiles):
                        rows.append((src, row0 + kk * 128))

                s_wdu8 = [a2(nc.sbuf_tensor(uname("wdu8"), [128, 8 * HH],
                                             U8)) for _ in range(2)]
                b_wdu8 = [Buf(uname("wdu8")) for _ in range(2)]
                b_wdc = [Buf(uname("wdc")) for _ in range(8)]
                for half in range(2):
                    for c8 in range(8):
                        em.dma("dS",
                               s_wdu8[c8 % 2][:, :]
                               .rearrange("p (i c) -> p i c", i=8),
                               dwd4[half, c8 * 8:(c8 + 1) * 8]
                               .rearrange("i p c -> p i c"),
                               writes=[b_wdu8[c8 % 2]])
                        em.op("act", lambda a, c8=c8: a.activation(
                            s_wd[:, c8 * 8 * HH:(c8 + 1) * 8 * HH],
                            s_wdu8[c8 % 2][:, :], AF.Copy, bias=-128.0),
                            reads=[b_wdu8[c8 % 2]], writes=[b_wdc[c8]])

                    for ts in range(nT_loc):
                        sl = ts % 2
                        for icb in range(4):
                            HB = s_hts[(ts * 4 + icb) % 3]
                            BHB = b_hts[(ts * 4 + icb) % 3]
                            em.dma("dA",
                                   HB[:, :].rearrange("p (i t) -> p i t", i=16),
                                   dh4[ts, icb * 16:(icb + 1) * 16, :, :]
                                   .rearrange("i p t -> p i t"), writes=[BHB])
                            em.dep_wait("pe", [BHB, b_wdc[2 * icb],
                                         b_wdc[2 * icb + 1]],
                                        [b_pD[sl]] if icb == 0 else [])
                            last_ins = None
                            for tg in range(2):
                                for il in range(16):
                                    ic = icb * 16 + il
                                    last_ins = nc.tensor.matmul(
                                        pD[sl][:, tg * 512:(tg + 1) * 512],
                                        HB[:, il * 128:(il + 1) * 128],
                                        s_wd[:, ic * HH + tg * 512:
                                             ic * HH + (tg + 1) * 512],
                                        start=(icb == 0 and il == 0),
                                        stop=(icb == 3 and il == 15))
                            em._finish(last_ins, "pe", 1,
                                       [BHB, b_wdc[2 * icb],
                                        b_wdc[2 * icb + 1]],
                                       [b_pD[sl]] if icb == 3 else [])
                        src, row0 = rows[ts]
                        em.dma("dA", s_xr[sl][:, :],
                               src[row0:row0 + 128, half * HH:(half + 1) * HH],
                               writes=[b_xr[sl]])
                        em.op("dve", lambda v, sl=sl, half=half: v.tensor_tensor(
                            s_t1[sl][:, :], pD[sl][:, :],
                            s_dsb[:, half * HH:(half + 1) * HH], ALU.mult),
                            reads=[b_pD[sl], b_dsb], writes=[b_t1[sl]])
                        em.op("dve", lambda v, sl=sl: v.scalar_tensor_tensor(
                            s_xo[sl][:, :], s_t1[sl][:, :], scale_half,
                            s_xr[sl][:, :], ALU.mult, ALU.add),
                            reads=[b_t1[sl], b_xr[sl]], writes=[b_xo[sl]])
                        em.dma("dV",
                               dst[ts * 128:(ts + 1) * 128,
                                   half * HH:(half + 1) * HH],
                               s_xo[sl][:, :], reads=[b_xo[sl]])
            em.barrier()

        with ExitStack() as ph:
            alP = ph.enter_context
            xnT = alP(nc.sbuf_tensor("xn1T", [128, NH * TT], BF16))
            bxnT = [Buf(f"xn1T{i}") for i in range(nT)]
            ln_srcs = ([(dx_pre, 0, nPre)] if nPre else []) + [(dx_own, 0, nTo)]
            g_ln = ln_transpose(alP, ln_srcs, xnT, bxnT, TT)
            if stop <= 1:
                drain(g_ln)
                em.drain_dmas()
                return nc
            ffn_gu(alP, xnT, bxnT, TT, True, ln_gen=g_ln)
        em.barrier()
        if stop <= 2:
            em.drain_dmas()
            return nc
        ffn_down(None, TT, True,
                 ([(dx_pre, 0, nPre)] if nPre else []) + [(dx_own, 0, nTo)],
                 dx1, 0.5)
        if stop <= 3:
            em.drain_dmas()
            return nc

        with ExitStack() as ph:
            alA = ph.enter_context
            s_o = alA(nc.sbuf_tensor("s_o", [128, NH * T_OWN], BF16))
            b_o = Buf("o")
            s_rb = alA(nc.sbuf_tensor("s_rb", [128, T_OWN], F32))
            b_rr = Buf("rr")
            xph = ph.enter_context(ExitStack())
            alX = xph.enter_context
            xnT = alX(nc.sbuf_tensor("xn2T", [128, NH * TT], BF16))
            bxnT = [Buf(f"xn2T{i}") for i in range(nT)]
            s_kf = alX(nc.sbuf_tensor("s_kf", [128, TT], F32))
            b_kf = Buf("kf")
            s_qf = alX(nc.sbuf_tensor("s_qf", [128, T_OWN], BF16))
            b_qf = Buf("qf")
            qst = ExitStack()
            s_qraw = qst.enter_context(
                nc.sbuf_tensor("s_qraw", [128, 1024], BF16))
            b_qraw = [Buf(uname("qraw")) for _ in range(2)]

            with ExitStack() as ps:
                a2 = ps.enter_context
                g_ln = ln_transpose(a2, [(dx1, 0, nT)], xnT, bxnT, TT,
                                    tok_dst=dxn2, nxn=2)
                if stop <= 4:
                    drain(g_ln)
                    em.drain_dmas()
                    return nc
                s_wqk = a2(nc.sbuf_tensor("s_wqk", [128, 2 * H], BF16))
                b_wqk = Buf("wqk")
                pK = a2(nc.psum_tensor("pKq", [128, 1024], F32))
                b_pK = [Buf(uname("pKq"), excl=True) for i in range(2)]
                pW = a2(nc.psum_tensor("pWg", [128, 1024], F32))
                b_pW = [Buf(f"pWg{i}", excl=True) for i in range(2)]
                pS = [a2(nc.psum_tensor(uname("pSm2"), [128, 128], BF16))
                      for _ in range(2)]
                b_pS = [Buf(uname("pSm2"), excl=True) for _ in range(2)]
                s_qe = a2(nc.sbuf_tensor("s_qe", [128, 260], F32))
                b_qe = [Buf(uname("qe")) for _ in range(2)]
                s_qb = a2(nc.sbuf_tensor("s_qb", [128, 256], BF16))
                b_qb = [Buf(uname("qb")) for _ in range(2)]
                s_wigb = [a2(nc.sbuf_tensor(uname("wigb"), [128, H], BF16))
                          for _ in range(2)]
                b_wigb = [Buf(uname("wigb")) for _ in range(2)]
                s_gwo = [a2(nc.sbuf_tensor(uname("gwo"), [128, 512], BF16))
                         for _ in range(2)]
                b_gwo = [Buf(uname("gwo")) for _ in range(2)]

                for wi, wsrc_ap in enumerate((dwq, dwk)):
                    w3 = wsrc_ap.rearrange("(h p) d -> p h d", p=128)
                    for part in range(2):
                        em.dma("dP",
                               s_wqk[:, (2 * wi + part) * (H // 2):
                                     (2 * wi + part + 1) * (H // 2)]
                               .rearrange("p (h d) -> p h d", h=NH // 2),
                               w3[:, part * (NH // 2):(part + 1) * (NH // 2), :],
                               writes=[b_wqk])

                lnd = [0]
                pKu = [0]

                def proj(stat_off, o_tok, wd, tiles):
                    sl = pKu[0] % 2
                    pKu[0] += 1
                    pap = pK[:, sl * 512: sl * 512 + wd]
                    mms = [(pap, s_wqk[:, stat_off + hc * 128:
                                       stat_off + (hc + 1) * 128],
                            xnT[:, hc * TT + o_tok: hc * TT + o_tok + wd])
                           for hc in range(NH)]
                    em.mm_plan([(b_pK[sl], mms, tiles)], reads=[b_wqk])
                    return pap, b_pK[sl]

                def softmax_chunk(c):
                    oo = c * 512
                    qsl = c % 2
                    bqr = b_qraw[qsl]
                    pap, pbuf = proj(0, T_PRE + oo, 512,
                                     bxnT[(T_PRE + oo) // 128:
                                          (T_PRE + oo + 512) // 128])
                    em.op("dve", lambda v, qsl=qsl, pap=pap: v.tensor_copy(
                        s_qraw[:, qsl * 512:qsl * 512 + 512], pap),
                        reads=[pbuf], writes=[bqr])
                    for tt in range(4 * c, 4 * c + 4):
                        sl2 = tt % 2
                        qo = qsl * 512 + (tt - 4 * c) * 128
                        em.op("pe", lambda t, qo=qo, sl2=sl2: t.transpose(
                            pS[sl2][:, :], s_qraw[:, qo:qo + 128],
                            s_ident[:, :]), reads=[bqr, b_c],
                            writes=[b_pS[sl2]])
                        QT = pS[sl2][:, :]
                        bqt = b_pS[sl2]
                        QE = s_qe[:, sl2 * 130:(sl2 + 1) * 130]
                        bqe, bqb = b_qe[sl2], b_qb[sl2]
                        QB = s_qb[:, sl2 * 128:(sl2 + 1) * 128]
                        em.op("dve", lambda v, QT=QT, QE=QE: v.reduce_max(
                            QE[:, 128:129], QT, AX.X, negate=True),
                            reads=[bqt], writes=[bqe])
                        em.op("act", lambda a, QT=QT, QE=QE: a.activation(
                            QE[:, 0:128], QT, AF.Exp,
                            bias=QE[:, 128:129], accum_out=QE[:, 129:130]),
                            reads=[bqt, bqe], writes=[bqe])
                        em.op("dve", lambda v, QE=QE: v.reciprocal(
                            QE[:, 128:129], QE[:, 129:130]),
                            reads=[bqe], writes=[bqe])
                        em.op("dve", lambda v, QE=QE, QB=QB: v.tensor_scalar(
                            QB, QE[:, 0:128], QE[:, 128:129], None,
                            ALU.mult), reads=[bqe], writes=[bqb])
                        em.op("pe", lambda t, QB=QB, sl2=sl2: t.transpose(
                            pS[sl2][:, :], QB, s_ident[:, :]),
                            reads=[bqb, b_c], writes=[b_pS[sl2]])
                        em.op("dve", lambda v, tt=tt, sl2=sl2: v.tensor_copy(
                            s_qf[:, tt * 128:(tt + 1) * 128], pS[sl2][:, :]),
                            reads=[b_pS[sl2]], writes=[b_qf])

                wig_pp = [0]

                def wig_pass(c0, nck):
                    for cb in range(NH):
                        sl = cb % 2
                        em.dma("dS", s_wigb[sl][:, :], dwbfa[1, cb],
                               writes=[b_wigb[sl]])
                        for c in range(c0, c0 + nck):
                            oo = c * 512
                            gi = wig_pp[0] % 2
                            wig_pp[0] += 1
                            pap = pW[:, gi * 512:gi * 512 + 512]
                            pbuf = b_pW[gi]
                            mms = [(pap,
                                    s_wigb[sl][:, hc * 128:(hc + 1) * 128],
                                    xnT[:, hc * TT + T_PRE + oo:
                                        hc * TT + T_PRE + oo + 512])
                                   for hc in range(NH)]
                            em.mm_plan([(pbuf, mms,
                                         bxnT[(T_PRE + oo) // 128:
                                              (T_PRE + oo + 512) // 128])],
                                       reads=[b_wigb[sl]])
                            em.op("act", lambda a, pap=pap, sl=sl:
                                  a.activation(s_gwo[sl][:, :], pap, AF.Copy),
                                  reads=[pbuf], writes=[b_gwo[sl]])
                            em.dma("dV", dgw[c, cb], s_gwo[sl][:, :],
                                   reads=[b_gwo[sl]])

                nck_all = nTo // 4
                done_c = 0
                groups = tgroups(TT)
                for ng, (o, wd) in enumerate(groups):
                    drive(g_ln, lnd, (o + wd + 127) // 128)
                    pap, pbuf = proj(H, o, wd, bxnT[o // 128:
                                                    (o + wd + 127) // 128])
                    em.op("act", lambda a, o=o, wd=wd, pap=pap: a.activation(
                        s_kf[:, o:o + wd], pap, AF.Sigmoid),
                        reads=[pbuf], writes=[b_kf])
                    avail_c = min(max(0, o + wd - T_PRE) // 512, nck_all)
                    while done_c < avail_c:
                        softmax_chunk(done_c)
                        wig_pass(done_c, 1)
                        done_c += 1
                drain(g_ln)

            em.barrier()
            qst.close()
            if stop <= 5:
                em.drain_dmas()
                return nc
            with ExitStack() as ps:
                a2 = ps.enter_context
                s_S = [a2(nc.sbuf_tensor(uname("s_S"), [128, H], F32))
                       for _ in range(2)]
                b_S = [Buf(uname("S")) for _ in range(2)]
                s_Sb = [a2(nc.sbuf_tensor(uname("s_Sb"), [128, H], BF16))
                        for _ in range(2)]
                b_Sb = [Buf(uname("Sb")) for _ in range(2)]
                s_pr = [a2(nc.sbuf_tensor(uname("s_pr"), [128, 132], F32))
                        for _ in range(3)]
                b_pr = [Buf(uname("pr")) for _ in range(3)]
                s_pe = [a2(nc.sbuf_tensor(uname("s_pe"), [128, 512], BF16))
                        for _ in range(2)]
                b_pe = [Buf(uname("pe_")) for _ in range(2)]
                s_pb = [a2(nc.sbuf_tensor(uname("s_pb"), [128, 1024], BF16))
                        for _ in range(3)]
                b_pb = [Buf(uname("pb")) for _ in range(3)]
                b_pbm = [Buf(uname("pbm")) for _ in range(3)]
                s_v = [a2(nc.sbuf_tensor(uname("s_v"), [128, H], BF16))
                       for _ in range(3)]
                b_v = [Buf(uname("v")) for _ in range(3)]
                s_rsq = a2(nc.sbuf_tensor("s_rsq", [128, H], BF16))
                b_rsq = Buf("rsq")
                pO = a2(nc.psum_tensor("pOs", [128, 2048], F32))
                b_pO = [Buf(f"pO{i}", excl=True) for i in range(2)]
                pM = a2(nc.psum_tensor("pMs", [128, 1024], F32))
                b_pM = Buf("pM", excl=True)
                pAT = a2(nc.psum_tensor("pATs", [128, 128], F32))
                b_pAT = Buf("pAT", excl=True)
                pTx = a2(nc.psum_tensor("pTx", [128, 1024], BF16))
                b_pTx = Buf("pTx", excl=True)

                em.op("dve", lambda v: v.memset(s_S[0][:, :], 0.0), writes=[b_S[0]])
                for sbi in range(2):
                    em.op("dve", lambda v, sbi=sbi: v.memset(
                        s_Sb[sbi][:, :], 0.0), writes=[b_Sb[sbi]])
                o3 = s_o[:, :].rearrange("p (cx t) -> p cx t", t=T_OWN)

                def prep_a(p):
                    pi = p % 3
                    own = p >= nPre
                    PR, BPR = s_pr[pi], b_pr[pi]
                    PEX, BPE = s_pe[p % 2], b_pe[p % 2]
                    PB, BPB = s_pb[pi], b_pb[pi]
                    em.op("act", lambda a: a.activation(
                        PR[:, 0:128], s_kf[:, p * 128:(p + 1) * 128], AF.Ln,
                        scale=-1.0, bias=1.0), reads=[b_kf], writes=[BPR])
                    em.op("dve", lambda v: v.tensor_tensor_scan(
                        PR[:, 0:128], PR[:, 0:128], PR[:, 0:128], 0.0,
                        ALU.add, ALU.bypass), reads=[BPR], writes=[BPR])
                    em.op("dve", lambda v: v.tensor_scalar(
                        PR[:, 128:129], PR[:, 63:64], -1.0, None, ALU.mult),
                        reads=[BPR], writes=[BPR])
                    em.op("act", lambda a: a.activation(
                        PR[:, 130:131], PR[:, 127:128], AF.Exp),
                        reads=[BPR], writes=[BPR])
                    em.op("act", lambda a: a.activation(
                        PEX[:, 256:384], PR[:, 0:128], AF.Exp, scale=-1.0,
                        bias=PR[:, 127:128]), reads=[BPR], writes=[BPE])
                    if own:
                        po = p - nPre
                        em.op("act", lambda a: a.activation(
                            PEX[:, 0:128], PR[:, 0:128], AF.Exp,
                            bias=PR[:, 128:129]), reads=[BPR], writes=[BPE])
                        em.op("act", lambda a: a.activation(
                            PEX[:, 128:256], PR[:, 0:128], AF.Exp),
                            reads=[BPR], writes=[BPE])
                        em.op("act", lambda a: a.activation(
                            PEX[:, 384:512], PR[:, 0:128], AF.Exp, scale=-1.0,
                            bias=PR[:, 63:64]), reads=[BPR], writes=[BPE])
                        em.op("dve", lambda v, po=po: v.tensor_tensor(
                            PB[:, 0:128], s_qf[:, po * 128:(po + 1) * 128],
                            PEX[:, 0:128], ALU.mult), reads=[b_qf, BPE],
                            writes=[BPB])
                        em.op("dve", lambda v, po=po: v.tensor_tensor(
                            PB[:, 128:256], s_qf[:, po * 128:(po + 1) * 128],
                            PEX[:, 128:256], ALU.mult), reads=[b_qf, BPE],
                            writes=[BPB])
                        em.op("dve", lambda v: v.tensor_scalar(
                            PB[:, 896:1024], PB[:, 128:256],
                            s_pr[(p - 1) % 3][:, 130:131], None, ALU.mult),
                            reads=[BPB, b_pr[(p - 1) % 3]], writes=[BPB])
                        em.op("dve", lambda v: v.tensor_tensor(
                            PB[:, 256:384], s_kf[:, p * 128:(p + 1) * 128],
                            PEX[:, 384:512], ALU.mult), reads=[b_kf, BPE],
                            writes=[BPB])

                def prep_b1(p):
                    pi = p % 3
                    PEX, BPE = s_pe[p % 2], b_pe[p % 2]
                    PB, BPB = s_pb[pi], b_pb[pi]
                    if p < nT - 1:
                        em.op("dve", lambda v: v.tensor_tensor(
                            PB[:, 384:512], s_kf[:, p * 128:(p + 1) * 128],
                            PEX[:, 256:384], ALU.mult),
                            reads=[b_kf, BPE], writes=[BPB])
                        em.op("pe", lambda t: t.transpose(
                            pTx[:, 0:128], PB[:, 384:512], s_ident[:, :]),
                            reads=[BPB, b_c], writes=[b_pTx])
                        em.op("act", lambda a: a.activation(
                            PB[:, 512:640], pTx[:, 0:128], AF.Copy),
                            reads=[b_pTx], writes=[BPB])

                def prep_b2(p):
                    pi = p % 3
                    V, BV = s_v[pi], b_v[pi]
                    em.dma("dS", V[:, :], dxn2[p], writes=[BV])

                def math_a(p):
                    pi = p % 3
                    own = p >= nPre
                    if not own:
                        return
                    PB, BPB = s_pb[pi], b_pb[pi]
                    PBv, BPBv = s_pb[(p - 1) % 3], b_pb[(p - 1) % 3]
                    if True:
                        em.op("dve", lambda v: v.memset(
                            PB[64:128, 640:704], 0.0), writes=[b_pbm[pi]])
                        em.op("pe", lambda t: t.matmul(
                            pM[:, 0:128], PBv[:, 384:512], PB[:, 128:256],
                            start=True, stop=True), reads=[BPBv, BPB],
                            writes=[b_pM])
                        em.op("dve", lambda v: v.tensor_copy(
                            PB[:, 768:896], pM[:, 0:128]),
                            reads=[b_pM], writes=[BPB])
                        em.op("pe", lambda t: t.matmul(
                            pAT[0:64, :], PB[:, 256:320], PB[:, 0:128],
                            start=True, stop=True), reads=[BPB], writes=[b_pAT])
                        em.op("pe", lambda t: t.matmul(
                            pAT[64:128, 64:128], PB[:, 320:384], PB[:, 64:128],
                            start=True, stop=True), reads=[BPB], writes=[b_pAT])
                        em.op("dve", lambda v: v.tensor_tensor(
                            PB[0:64, 640:768], pAT[0:64, :], s_umask[0:64, :],
                            ALU.min), reads=[b_pAT, b_c], writes=[b_pbm[pi]])
                        em.op("dve", lambda v: v.tensor_tensor(
                            PB[64:128, 704:768], pAT[64:128, 64:128],
                            s_umask[64:128, 64:128], ALU.min),
                            reads=[b_pAT, b_c], writes=[b_pbm[pi]])

                def math_b(p):
                    pi = p % 3
                    own = p >= nPre
                    po = p - nPre
                    PR, BPR = s_pr[pi], b_pr[pi]
                    PB, BPB = s_pb[pi], b_pb[pi]
                    V, BV = s_v[pi], b_v[pi]
                    S_in, bS_in = s_S[p % 2], b_S[p % 2]
                    S_out, bS_out = s_S[(p + 1) % 2], b_S[(p + 1) % 2]
                    SB, bSB = s_Sb[p % 2], b_Sb[p % 2]
                    if own:
                        PBv, BPBv = s_pb[(p - 1) % 3], b_pb[(p - 1) % 3]
                        Vv, BVv = s_v[(p - 1) % 3], b_v[(p - 1) % 3]
                        for hf in range(2):
                            em.dep_wait("pe", [BPB, BV, BVv, bSB], [b_pO[hf]])
                            for dd in range(NH // 2):
                                dvc = hf * (NH // 2) + dd
                                nc.tensor.matmul(
                                    pO[:, dvc * 128:(dvc + 1) * 128],
                                    Vv[:, dvc * 128:(dvc + 1) * 128],
                                    PB[:, 768:896], start=True, stop=False)
                                nc.tensor.matmul(
                                    pO[:, dvc * 128:(dvc + 1) * 128],
                                    SB[:, dvc * 128:(dvc + 1) * 128],
                                    PB[:, 896:1024], start=False, stop=False)
                                em.dep_wait("pe", [b_pbm[pi]], [])
                                i2 = nc.tensor.matmul(
                                    pO[:, dvc * 128:(dvc + 1) * 128],
                                    V[:, dvc * 128:(dvc + 1) * 128],
                                    PB[:, 640:768], start=False, stop=True)
                                if dd == NH // 2 - 1:
                                    em._finish(i2, "pe", 1,
                                               [BPB, BV, BVv, bSB,
                                                b_pbm[pi]],
                                               [b_pO[hf]])
                            if hf == 0:
                                em.op("act", lambda a, po=po: a.activation(
                                    o3[:, 0:8, po * 128:(po + 1) * 128],
                                    pO[:, 0:1024]
                                    .rearrange("p (cx t) -> p cx t", t=128),
                                    AF.Copy), reads=[b_pO[0]], writes=[b_o])
                            else:
                                em.op("dve", lambda v, po=po: v.tensor_copy(
                                    o3[:, 8:16, po * 128:(po + 1) * 128],
                                    pO[:, 1024:2048]
                                    .rearrange("p (cx t) -> p cx t", t=128)),
                                    reads=[b_pO[1]], writes=[b_o])
                def math_m(p):
                    pi = p % 3
                    PR, BPR = s_pr[pi], b_pr[pi]
                    PB, BPB = s_pb[pi], b_pb[pi]
                    V, BV = s_v[pi], b_v[pi]
                    S_in, bS_in = s_S[p % 2], b_S[p % 2]
                    S_out, bS_out = s_S[(p + 1) % 2], b_S[(p + 1) % 2]
                    SB, bSB = s_Sb[p % 2], b_Sb[p % 2]
                    if p >= nT - 2:
                        return
                    for mh in range(2):
                        pMap = pM[:, 0:1024] if mh == 0 else pO[:, 0:1024]
                        bpMm = b_pM if mh == 0 else b_pO[0]
                        em.dep_wait("pe", [BPB, BV], [bpMm])
                        for tg in range(2):
                            ins = nc.tensor.matmul(
                                pMap[:, tg * 512:(tg + 1) * 512],
                                PB[:, 512:640],
                                V[:, mh * 1024 + tg * 512:
                                  mh * 1024 + (tg + 1) * 512],
                                start=True, stop=True)
                            if tg == 1:
                                em._finish(ins, "pe", 1, [BPB, BV], [bpMm])
                        em.op("dve", lambda v, mh=mh, pMap=pMap:
                              v.scalar_tensor_tensor(
                            S_out[:, mh * 1024:(mh + 1) * 1024],
                            S_in[:, mh * 1024:(mh + 1) * 1024],
                            PR[:, 130:131], pMap, ALU.mult, ALU.add),
                            reads=[bpMm, BPR, bS_in], writes=[bS_out])
                        em.op("dve", lambda v, mh=mh: v.tensor_copy(
                            SB[:, mh * 1024:(mh + 1) * 1024],
                            S_out[:, mh * 1024:(mh + 1) * 1024]),
                            reads=[bS_out], writes=[bSB])

                def rstat(po):
                    em.op("act", lambda a, po=po: a.activation(
                        s_rsq[:, :], o3[:, :, po * 128:(po + 1) * 128],
                        AF.Square), reads=[b_o], writes=[b_rsq])
                    em.dep_wait("pe", [b_rsq, b_c], [b_pAT])
                    for dvc in range(NH):
                        ins = nc.tensor.matmul(
                            pAT[0:1, 0:128], s_ones1[:, :],
                            s_rsq[:, dvc * 128:(dvc + 1) * 128],
                            start=(dvc == 0), stop=(dvc == NH - 1))
                        if dvc == NH - 1:
                            em._finish(ins, "pe", 1, [b_rsq, b_c], [b_pAT])
                    em.op("act", lambda a, po=po: a.activation(
                        s_rb[0:1, po * 128:(po + 1) * 128], pAT[0:1, 0:128],
                        AF.Ln, scale=s_cst[0:1, 1:2], bias=s_cst[0:1, 0:1]),
                        reads=[b_pAT, b_c], writes=[b_rr])
                    em.op("act", lambda a, po=po: a.activation(
                        s_rb[0:1, po * 128:(po + 1) * 128],
                        s_rb[0:1, po * 128:(po + 1) * 128], AF.Exp, scale=-0.5),
                        reads=[b_rr], writes=[b_rr])

                prep_a(0)
                prep_b1(0)
                prep_b2(0)
                prep_a(1)
                prep_b1(1)
                prep_b2(1)
                for p in range(nT):
                    math_a(p)
                    if p + 2 < nT:
                        prep_a(p + 2)
                    math_b(p)
                    if p + 2 < nT:
                        prep_b1(p + 2)
                    math_m(p)
                    if p >= nPre:
                        rstat(p - nPre)
                    if p + 2 < nT:
                        prep_b2(p + 2)

            xph.close()
            em.barrier()
            if stop <= 6:
                em.drain_dmas()
                return nc
            s_wo = alA(nc.sbuf_tensor("s_wo", [128, NH * H], BF16))
            b_wo = Buf("wo")
            for cc in range(NH):
                em.dma("dV", s_wo[:, cc * H:(cc + 1) * H], dwbfa[2, cc],
                       writes=[b_wo])
            with ExitStack() as ps:
                a2 = ps.enter_context
                b_rb = b_rr
                pGo = a2(nc.psum_tensor("pGo", [128, 2048], F32))
                b_pGo = [Buf(f"pGo{i}", excl=True) for i in range(4)]
                for (o, wd) in tgroups(T_OWN):
                    em.op("pe", lambda t, o=o, wd=wd: t.matmul(
                        pGo[:, 0:wd], s_onesr[0:1, :], s_rb[0:1, o:o + wd],
                        start=True, stop=True), reads=[b_rr, b_c],
                        writes=[b_pGo[0]])
                    em.op("dve", lambda v, o=o, wd=wd: v.tensor_copy(
                        s_rb[:, o:o + wd], pGo[:, 0:wd]),
                        reads=[b_pGo[0]], writes=[b_rb])

                s_wb2 = [a2(nc.sbuf_tensor(uname("s_wb2"), [128, 2048], BF16))
                         for _ in range(2)]
                b_wb2 = [Buf(uname("wb2")) for _ in range(2)]
                s_gwi = [a2(nc.sbuf_tensor(uname("gwi"), [128, 1024], BF16))
                         for _ in range(2)]
                b_gwi = [Buf(uname("gwi")) for _ in range(2)]
                s_gof = a2(nc.sbuf_tensor("s_gof", [128, 1024], F32))
                b_gof = Buf("gof")
                s_sg = a2(nc.sbuf_tensor("s_sg", [128, 1024], BF16))
                b_sg = Buf("sg")
                s_yb = a2(nc.sbuf_tensor("s_yb", [128, NH * min(1024, T_OWN)],
                                          BF16))
                b_yb = Buf("yb")
                s_xr2 = a2(nc.sbuf_tensor("s_xr2", [128, H], F32))
                b_xr2 = Buf("xr2")
                s_xo2 = a2(nc.sbuf_tensor("s_xo2", [128, H], F32))
                b_xo2 = Buf("xo2")

                SW = min(1024, T_OWN)
                for c in range(T_OWN // SW):
                    for cb in range(NH):
                        slw = cb % 2
                        WB, BWB = s_wb2[slw], b_wb2[slw]
                        em.dma("dS", WB[:, :], dwbfa[0, cb], writes=[BWB])
                        for g2 in range(SW // 512):
                            em.dma("dA", s_gwi[slw][:, g2 * 512:
                                                    g2 * 512 + 512],
                                   dgw[(c * SW) // 512 + g2, cb],
                                   writes=[b_gwi[slw]])
                        g0 = (2 * cb) % 4
                        for g2 in range(SW // 512):
                            gi = g0 + g2
                            pap = pGo[:, gi * 512:(gi + 1) * 512]
                            mms = [(pap, WB[:, dvc * 128:(dvc + 1) * 128],
                                    s_o[:, dvc * T_OWN + c * SW + g2 * 512:
                                        dvc * T_OWN + c * SW + g2 * 512
                                        + 512])
                                   for dvc in range(NH)]
                            em.mm_plan([(b_pGo[gi], mms)], reads=[BWB, b_o])
                        gbk = [b_pGo[g0 + g2] for g2 in range(SW // 512)]
                        em.op("dve", lambda v, g0=g0, slw=slw:
                              v.tensor_tensor(
                                  s_gof[:, 0:SW],
                                  pGo[:, g0 * 512:g0 * 512 + SW],
                                  s_gwi[slw][:, 0:SW], ALU.add),
                              reads=gbk + [b_gwi[slw]],
                              writes=[b_gof])
                        em.op("act", lambda a: a.activation(
                            s_sg[:, 0:SW], s_gof[:, 0:SW], AF.Silu),
                            reads=[b_gof], writes=[b_sg])
                        em.op("dve", lambda v, cb=cb, c=c:
                              v.scalar_tensor_tensor(
                                  s_gof[:, 0:SW],
                                  s_o[:, cb * T_OWN + c * SW:
                                      cb * T_OWN + c * SW + SW],
                                  s_gnw[:, cb:cb + 1],
                                  s_rb[:, c * SW:c * SW + SW],
                                  ALU.mult, ALU.mult),
                              reads=[b_o, b_rb, b_c], writes=[b_gof])
                        em.op("dve", lambda v, cb=cb: v.tensor_tensor(
                            s_yb[:, cb * SW:(cb + 1) * SW], s_gof[:, 0:SW],
                            s_sg[:, 0:SW], ALU.mult),
                            reads=[b_gof, b_sg], writes=[b_yb])
                    for sub in range(SW // 128):
                        t0 = c * SW + sub * 128
                        em.dma("dA", s_xr2[:, :],
                               dx1[T_PRE + t0: T_PRE + t0 + 128, :],
                               writes=[b_xr2])
                        for half in range(2):
                            bk = [b_pGo[2 * half], b_pGo[2 * half + 1]]
                            em.dep_wait("pe", [b_yb, b_wo], bk)
                            for cc in range(NH):
                                for tg in range(2):
                                    ins = nc.tensor.matmul(
                                        pGo[:, half * 1024 + tg * 512:
                                            half * 1024 + (tg + 1) * 512],
                                        s_yb[:, cc * SW + sub * 128:
                                             cc * SW + sub * 128 + 128],
                                        s_wo[:, cc * H + half * 1024 + tg * 512:
                                             cc * H + half * 1024 +
                                             (tg + 1) * 512],
                                        start=(cc == 0), stop=(cc == NH - 1))
                                    if cc == NH - 1 and tg == 1:
                                        em._finish(ins, "pe", 1,
                                                   [b_yb, b_wo], bk)
                            em.op("dve", lambda v, half=half:
                                  v.tensor_tensor(
                                      s_xo2[:, half * 1024:(half + 1) * 1024],
                                      pGo[:, half * 1024:(half + 1) * 1024],
                                      s_xr2[:, half * 1024:(half + 1) * 1024],
                                      ALU.add),
                                  reads=bk + [b_xr2], writes=[b_xo2])
                        em.dma("dV", dx2[t0:t0 + 128, :],
                               s_xo2[:, :], reads=[b_xo2])
                em.barrier()

        em.barrier()
        if stop <= 7:
            em.drain_dmas()
            return nc
        if stop <= 8:
            em.drain_dmas()
            return nc
        with ExitStack() as ph:
            alP = ph.enter_context
            xnT = alP(nc.sbuf_tensor("xn3T", [128, NH * T_OWN], BF16))
            bxnT = [Buf(f"xn3T{i}") for i in range(nTo)]
            g_ln = ln_transpose(alP, [(dx2, 0, nTo)], xnT, bxnT, T_OWN)
            ffn_gu(alP, xnT, bxnT, T_OWN, False, ln_gen=g_ln)
        em.barrier()
        ffn_down(None, T_OWN, False, [(dx2, 0, nTo)], dout, 0.5)

        em.drain_dmas()
    return nc


def _layout_inputs(L0_attn_hidden_states, L0_attn_Wq, L0_attn_Wk, L0_attn_Wog,
                   L0_attn_Wig, L0_attn_Wo, L0_attn_g_norm_w, L0_ffn_gate_w,
                   L0_ffn_gate_scale, L0_ffn_up_w, L0_ffn_up_scale,
                   L0_ffn_down_w, L0_ffn_down_scale, T_OWN, T_PRE):
    x = np.asarray(L0_attn_hidden_states)
    B, L, _ = x.shape
    f32 = np.float32

    def cont(a, dt=f32):
        return np.ascontiguousarray(a, dtype=dt)

    wg4 = cont(np.transpose(np.asarray(L0_ffn_gate_w).reshape(NI, 128, NH, 128),
                            (0, 3, 2, 1)).reshape(NI, 128, H), np.uint8)
    wu4 = cont(np.transpose(np.asarray(L0_ffn_up_w).reshape(NI, 128, NH, 128),
                            (0, 3, 2, 1)).reshape(NI, 128, H), np.uint8)
    wd4 = cont(np.transpose(np.asarray(L0_ffn_down_w).reshape(NH, 128, NI, 128),
                            (2, 3, 0, 1)).reshape(NI, 128, H)
               .reshape(NI, 128, 2, H // 2).transpose(2, 0, 1, 3), np.uint8)
    gs2 = cont(np.asarray(L0_ffn_gate_scale).reshape(NI, 128).T)
    us2 = cont(np.asarray(L0_ffn_up_scale).reshape(NI, 128).T)
    dsb = cont(np.broadcast_to(np.asarray(L0_ffn_down_scale), (128, H)))
    gnw2 = cont(np.asarray(L0_attn_g_norm_w).reshape(NH, 128).T)
    wq2 = cont(np.asarray(L0_attn_Wq).T)
    wk2 = cont(np.asarray(L0_attn_Wk).T)
    wog4 = cont(np.transpose(np.asarray(L0_attn_Wog).reshape(NH, 128, NH, 128),
                             (0, 2, 3, 1)))
    wig4 = cont(np.transpose(np.asarray(L0_attn_Wig).reshape(NH, 128, NH, 128),
                             (0, 2, 3, 1)))
    wo3 = cont(np.asarray(L0_attn_Wo).T.reshape(NH, 128, H))
    ident = np.eye(128, dtype=f32).astype(ml_dtypes.bfloat16)
    jj, ii = np.meshgrid(np.arange(128), np.arange(128), indexing="ij")
    umask = np.where(jj <= ii, BIGMASK, 0.0).astype(f32)
    csts = np.broadcast_to(np.array([EPS, 1.0 / H], dtype=f32), (128, 2)).copy()

    shared = dict(wg4=wg4, wu4=wu4, wd4=wd4, gs2=gs2, us2=us2, dsb=dsb,
                  gnw2=gnw2, wq2=wq2, wk2=wk2, wog4=wog4, wig4=wig4, wo3=wo3,
                  ident=ident, umask=umask, csts=csts)
    in_maps = []
    halves = L // T_OWN
    n_cores = B * halves
    for c in range(n_cores):
        b, hh = c // halves, c % halves
        t0 = hh * T_OWN
        x_own = cont(x[b, t0:t0 + T_OWN, :])
        if t0 == 0:
            x_pre = np.zeros((T_PRE, H), dtype=f32)
        else:
            x_pre = cont(x[b, t0 - T_PRE:t0, :])
        in_maps.append(dict(shared, x_own=x_own, x_pre=x_pre))
    return in_maps, n_cores


def kernel(**inputs):
    T_OWN, T_PRE = 2048, 128
    in_maps, n_cores = _layout_inputs(**inputs, T_OWN=T_OWN, T_PRE=T_PRE)
    nc = build_program(T_OWN, T_PRE)
    res = run_bass_kernel_spmd(nc, in_maps, core_ids=list(range(n_cores)))
    x = np.asarray(inputs["L0_attn_hidden_states"])
    B, L, _ = x.shape
    halves = L // T_OWN
    out = np.empty((B, L, H), dtype=np.float32)
    for c in range(n_cores):
        b, hh = c // halves, c % halves
        out[b, hh * T_OWN:(hh + 1) * T_OWN, :] = res.results[c]["out"]
    return out


# version 51
# speedup vs baseline: 1.0153x; 1.0153x over previous
import numpy as np
import ml_dtypes
import concourse.bass as bass
import concourse.mybir as mybir
from concourse.bass_utils import run_bass_kernel_spmd
from contextlib import ExitStack

F32 = mybir.dt.float32
BF16 = mybir.dt.bfloat16
I32 = mybir.dt.int32
U8 = mybir.dt.uint8
AF = mybir.ActivationFunctionType
ALU = mybir.AluOpType
AX = mybir.AxisListType

H = 2048
NH = H // 128
I_FFN = 8192
NI = I_FFN // 128
DK = 128
EPS = 1e-5
BIGMASK = 3e38


class Buf:
    __slots__ = ("name", "last_w", "readers", "excl")

    def __init__(self, name, excl=False):
        self.name = name
        self.last_w = None
        self.readers = []
        self.excl = excl


class Em:

    NQ = 8

    def __init__(self, nc, ctx):
        self.nc = nc
        names = ["pe", "act", "dve", "pool"]
        self.sem = {n: ctx.enter_context(nc.semaphore("sem_" + n)) for n in names}
        self.cnt = {n: 0 for n in names}
        for q in ("dS", "dA", "dP", "dV"):
            for i in range(self.NQ):
                n = f"{q}{i}"
                self.sem[n] = ctx.enter_context(nc.semaphore("sem_" + n))
                self.cnt[n] = 0
        self.qissue = {"dS": 0, "dA": 0, "dP": 0, "dV": 0}
        self.eng = {"pe": nc.tensor, "act": nc.scalar, "dve": nc.vector,
                    "pool": nc.gpsimd, "sync": nc.sync}
        self.qeng = {"dS": "sync", "dA": "act", "dP": "pool",
                     "dV": "pool"}
        self.waited = {n: {} for n in self.eng}

    def _wait(self, ename, sem_name, val):
        if val <= 0:
            return
        w = self.waited[ename]
        if w.get(sem_name, 0) >= val:
            return
        self.eng[ename].wait_ge(self.sem[sem_name], val)
        w[sem_name] = val

    def dep_wait(self, ename, reads, writes):
        need = {}

        def add(tag):
            if tag is not None and need.get(tag[0], 0) < tag[1]:
                need[tag[0]] = tag[1]
        for b in reads:
            add(b.last_w)
            if b.excl:
                for t in b.readers:
                    add(t)
        for b in writes:
            add(b.last_w)
            for t in b.readers:
                add(t)
        for s, v in need.items():
            self._wait(ename, s, v)

    def _finish(self, ins, sem_name, inc, reads, writes):
        ins.then_inc(self.sem[sem_name], inc)
        self.cnt[sem_name] += inc
        tag = (sem_name, self.cnt[sem_name])
        for b in writes:
            b.last_w = tag
            b.readers = []
        for b in reads:
            b.readers.append(tag)
        return ins

    def op(self, ename, fn, reads=(), writes=()):
        reads = list(reads)
        writes = list(writes)
        self.dep_wait(ename, reads, writes)
        ins = fn(self.eng[ename])
        return self._finish(ins, ename, 1, reads, writes)

    def dma(self, q, out_ap, in_ap, reads=(), writes=(), transpose=False):
        reads = list(reads)
        writes = list(writes)
        ename = self.qeng[q]
        j = self.qissue[q]
        self.qissue[q] += 1
        sname = f"{q}{j % self.NQ}"
        self._wait(ename, sname, self.cnt[sname])
        self.dep_wait(ename, reads, writes)
        if transpose:
            ins = self.eng[ename].dma_start_transpose(out_ap, in_ap)
        else:
            ins = self.eng[ename].dma_start(out_ap, in_ap)
        return self._finish(ins, sname, 16, reads, writes)

    def mm_plan(self, plan, reads=()):
        reads = list(reads)
        self.dep_wait("pe", reads, [])
        allg = []
        for entry in plan:
            buf, mms = entry[0], entry[1]
            greads = list(entry[2]) if len(entry) > 2 else []
            allg.append(greads)
            self.dep_wait("pe", greads, [buf])
            for k, (out_ap, l, r) in enumerate(mms):
                last = k == len(mms) - 1
                ins = self.nc.tensor.matmul(out_ap, l, r, start=(k == 0), stop=last)
                if last:
                    self._finish(ins, "pe", 1, [], [buf])
        tag = ("pe", self.cnt["pe"])
        for b in reads:
            b.readers.append(tag)
        for greads in allg:
            for b in greads:
                b.readers.append(tag)

    def barrier(self):
        for ename in self.eng:
            for s, v in self.cnt.items():
                self._wait(ename, s, v)

    def drain_dmas(self):
        for q in ("dS", "dA", "dP", "dV"):
            for i in range(self.NQ):
                n = f"{q}{i}"
                self._wait("sync", n, self.cnt[n])


def tgroups(t):
    out = []
    o = 0
    while o < t:
        w = min(512, t - o)
        out.append((o, w))
        o += w
    return out


_uid = [0]


def uname(base):
    _uid[0] += 1
    return f"{base}_{_uid[0]}"


def build_program(T_OWN, T_PRE, stop=99, scan_stop=99):
    TT = T_OWN + T_PRE
    nT = TT // 128
    nTo = T_OWN // 128
    nPre = T_PRE // 128
    nc = bass.Bass("TRN2", target_bir_lowering=False)

    dx_own = nc.dram_tensor("x_own", [T_OWN, H], F32, kind="ExternalInput").ap()
    dx_pre = nc.dram_tensor("x_pre", [T_PRE, H], F32, kind="ExternalInput").ap()
    dwg4 = nc.dram_tensor("wg4", [NI, 128, H], U8, kind="ExternalInput").ap()
    dwu4 = nc.dram_tensor("wu4", [NI, 128, H], U8, kind="ExternalInput").ap()
    dwd4 = nc.dram_tensor("wd4", [2, NI, 128, H // 2], U8,
                          kind="ExternalInput").ap()
    dgs = nc.dram_tensor("gs2", [128, NI], F32, kind="ExternalInput").ap()
    dus = nc.dram_tensor("us2", [128, NI], F32, kind="ExternalInput").ap()
    ddsb = nc.dram_tensor("dsb", [128, H], F32, kind="ExternalInput").ap()
    dgnw = nc.dram_tensor("gnw2", [128, NH], F32, kind="ExternalInput").ap()
    dwq = nc.dram_tensor("wq2", [H, DK], F32, kind="ExternalInput").ap()
    dwk = nc.dram_tensor("wk2", [H, DK], F32, kind="ExternalInput").ap()
    dwog = nc.dram_tensor("wog4", [NH, NH, 128, 128], F32, kind="ExternalInput").ap()
    dwig = nc.dram_tensor("wig4", [NH, NH, 128, 128], F32, kind="ExternalInput").ap()
    dwo = nc.dram_tensor("wo3", [NH, 128, H], F32, kind="ExternalInput").ap()
    dident = nc.dram_tensor("ident", [128, 128], BF16, kind="ExternalInput").ap()
    dumask = nc.dram_tensor("umask", [128, 128], F32, kind="ExternalInput").ap()
    dcsts = nc.dram_tensor("csts", [128, 2], F32, kind="ExternalInput").ap()

    dout = nc.dram_tensor("out", [T_OWN, H], F32, kind="ExternalOutput").ap()

    dx1 = nc.dram_tensor("x1s", [TT, H], F32, kind="Internal").ap()
    dx2 = nc.dram_tensor("x2s", [T_OWN, H], F32, kind="Internal").ap()
    dh4 = nc.dram_tensor("h4s", [nT, NI, 128, 128], BF16, kind="Internal").ap()
    dgw = nc.dram_tensor("gwig", [T_OWN // 512, NH, 128, 512], BF16,
                         kind="Internal").ap()
    dxn2 = nc.dram_tensor("xn2tok", [nT, 128, H], BF16, kind="Internal").ap()
    dwbfa = nc.dram_tensor("wbfa", [3, NH, 128, H], BF16, kind="Internal").ap()

    with ExitStack() as ctx:
        em = Em(nc, ctx)
        AL = ctx.enter_context

        s_ident = AL(nc.sbuf_tensor("s_ident", [128, 128], BF16))
        s_umask = AL(nc.sbuf_tensor("s_umask", [128, 128], F32))
        s_gs = AL(nc.sbuf_tensor("s_gs", [128, NI], F32))
        s_us = AL(nc.sbuf_tensor("s_us", [128, NI], F32))
        s_gnw = AL(nc.sbuf_tensor("s_gnw", [128, NH], F32))
        s_ones1 = AL(nc.sbuf_tensor("s_ones1", [128, 1], BF16))
        s_onesr = AL(nc.sbuf_tensor("s_onesr", [1, 128], F32))
        s_cst = AL(nc.sbuf_tensor("s_cst", [128, 2], F32))
        b_c = Buf("consts")
        for dst, src in [(s_ident, dident), (s_umask, dumask), (s_gs, dgs),
                         (s_us, dus), (s_gnw, dgnw), (s_cst, dcsts)]:
            em.dma("dS", dst[:, :], src, writes=[b_c])
        em.op("dve", lambda v: v.memset(s_ones1[:, :], 1.0), writes=[b_c])
        em.op("dve", lambda v: v.memset(s_onesr[:, :], 1.0), writes=[b_c])

        def ln_transpose(al, srcs, xnT, bxnT, TT_loc, tok_dst=None,
                         nxn=3):
            if True:
                a2 = al
                pT = [a2(nc.psum_tensor(uname("pT"), [128, 1024], BF16))
                      for _ in range(2)]
                bpT = [Buf(uname("bpT"), excl=True) for _ in range(2)]
                s_x = [a2(nc.sbuf_tensor(uname("lnx"), [128, H], F32))
                       for _ in range(2)]
                s_xn = [a2(nc.sbuf_tensor(uname("lnxn"), [128, H], BF16))
                        for _ in range(nxn)]
                s_st = a2(nc.sbuf_tensor(uname("lnst"), [128, 24], F32))
                bx = [Buf("lnx0"), Buf("lnx1")]
                bxn = [Buf(uname("lnxn")) for _ in range(3)]
                bstl = [Buf(uname("lnst")) for _ in range(3)]
                bsql = [Buf(uname("lnsq")) for _ in range(3)]
                xnT3 = xnT[:, :].rearrange("p (h t) -> p h t", t=TT_loc)
                tiles = []
                for (src, row0, ntiles) in srcs:
                    for kk in range(ntiles):
                        tiles.append(src[row0 + kk * 128: row0 + (kk + 1) * 128, :])

                def xload(ti):
                    em.dma("dA", s_x[ti % 2][:, :], tiles[ti],
                           writes=[bx[ti % 2]])

                def head(tj):
                    slh = tj % nxn
                    sth = s_st[:, slh * 8:slh * 8 + 8]
                    Xh, BXh = s_x[tj % 2], bx[tj % 2]
                    XNh, BXNh = s_xn[slh], bxn[slh]
                    em.op("dve", lambda v: v.reduce_sum(
                        sth[:, 0:1], Xh[:, :], AX.X), reads=[BXh],
                        writes=[bstl[slh]])
                    em.op("act", lambda a: a.activation(
                        XNh[:, :], Xh[:, :], AF.Square,
                        accum_out=sth[:, 1:2]),
                        reads=[BXh], writes=[BXNh, bsql[slh]])

                def _gen():
                    xload(0)
                    if len(tiles) > 1:
                        xload(1)
                    head(0)
                    for ti in range(len(tiles)):
                        sl = ti % nxn
                        c0 = sl * 8
                        X, BX = s_x[ti % 2], bx[ti % 2]
                        XN, BXN = s_xn[sl], bxn[sl]
                        bst = bstl[sl]
                        st = s_st[:, c0:c0 + 8]
                        bsq = bsql[sl]
                        if ti + 1 < len(tiles):
                            head(ti + 1)
                        em.op("dve", lambda v: v.tensor_scalar(
                            st[:, 2:3], st[:, 0:1], 1.0 / H, None, ALU.mult),
                            reads=[bst], writes=[bst])
                        em.op("dve", lambda v: v.tensor_tensor(
                            st[:, 3:4], st[:, 2:3], st[:, 2:3], ALU.mult),
                            reads=[bst], writes=[bst])
                        em.op("dve", lambda v: v.scalar_tensor_tensor(
                            st[:, 4:5], st[:, 1:2], 1.0 / H, st[:, 3:4],
                            ALU.mult, ALU.subtract), reads=[bst, bsq],
                            writes=[bst])
                        em.op("act", lambda a: a.activation(
                            st[:, 5:6], st[:, 4:5], AF.Ln,
                            bias=s_cst[:, 0:1]),
                            reads=[bst, b_c], writes=[bst])
                        em.op("act", lambda a: a.activation(
                            st[:, 6:7], st[:, 5:6], AF.Exp, scale=-0.5),
                            reads=[bst], writes=[bst])
                        em.op("dve", lambda v: v.scalar_tensor_tensor(
                            st[:, 7:8], st[:, 2:3], -1.0, st[:, 6:7],
                            ALU.mult, ALU.mult), reads=[bst], writes=[bst])
                        em.op("dve", lambda v: v.tensor_scalar(
                            XN[:, :], X[:, :], st[:, 6:7], st[:, 7:8],
                            ALU.mult, ALU.add), reads=[BX, bst], writes=[BXN])
                        if ti + 2 < len(tiles):
                            xload(ti + 2)
                        if tok_dst is not None:
                            em.dma("dV", tok_dst[ti], XN[:, :], reads=[BXN])
                        for half in range(2):
                            pb = bpT[half]
                            em.dep_wait("pe", [BXN, b_c], [pb])
                            for hh in range(8):
                                hc = half * 8 + hh
                                ins = nc.tensor.transpose(
                                    pT[half][:, hh * 128:(hh + 1) * 128],
                                    XN[:, hc * 128:(hc + 1) * 128], s_ident[:, :])
                                if hh == 7:
                                    em._finish(ins, "pe", 1, [BXN, b_c], [pb])
                            if half == 0:
                                em.op("act", lambda a, ti=ti: a.activation(
                                    xnT3[:, 0:8, ti * 128:(ti + 1) * 128],
                                    pT[0][:, :].rearrange(
                                        "p (h t) -> p h t", t=128),
                                    AF.Copy), reads=[pb], writes=[bxnT[ti]])
                            else:
                                em.op("dve", lambda v, ti=ti: v.tensor_copy(
                                    xnT3[:, 8:16, ti * 128:(ti + 1) * 128],
                                    pT[1][:, :].rearrange(
                                        "p (h t) -> p h t", t=128)),
                                    reads=[pb], writes=[bxnT[ti]])
                        yield ti
                return _gen()

        def drive(gen, done, need):
            if gen is None:
                return
            while done[0] < need:
                try:
                    next(gen)
                except StopIteration:
                    break
                done[0] += 1

        def drain(gen):
            if gen is not None:
                for _ in gen:
                    pass

        def ffn_gu(al, xnT, bxnT, TT_loc, first_call, ln_gen=None):
            tgs = tgroups(TT_loc)
            with ExitStack() as ps:
                a2 = ps.enter_context
                al = a2
                s_wraw = [a2(nc.sbuf_tensor(uname("wraw"), [128, H], U8))
                          for _ in range(2)]
                s_wbf = [a2(nc.sbuf_tensor(uname("wbf"), [128, H], BF16))
                         for _ in range(10)]
                s_silu = [a2(nc.sbuf_tensor(uname("silu"), [128, TT_loc], BF16))
                          for _ in range(5)]
                s_h = [a2(nc.sbuf_tensor(uname("hsb"), [128, TT_loc], BF16))
                       for _ in range(5)]
                b_wraw = [Buf(uname("wraw")) for _ in range(2)]
                b_wbf = [Buf(uname("wbf")) for _ in range(10)]
                b_silu = [Buf(uname("silu")) for _ in range(5)]
                b_h = [Buf(uname("hsb")) for _ in range(5)]
                pG = a2(nc.psum_tensor(uname("pG"), [128, 2048], F32))
                b_pG = [Buf(uname("pG"), excl=True) for _ in range(4)]
                pG2 = [a2(nc.psum_tensor(uname("pG2"), [128, 512], F32))
                       for _ in range(2)]
                b_pG2 = [Buf(uname("pG2"), excl=True) for _ in range(2)]

                if first_call:
                    s_af = [al(nc.sbuf_tensor(uname("caf"), [128, H], F32))] * 2
                    s_ab = [al(nc.sbuf_tensor(uname("cab"), [128, H], BF16))] * 2
                    b_af = [Buf(uname("caf"))] * 2
                    b_ab = [Buf(uname("cab"))] * 2
                    aconv = []
                    for wi2, wd4 in enumerate((dwog, dwig)):
                        for cb in range(NH):
                            aconv.append((wi2, cb,
                                          wd4[cb].rearrange("d p c -> p d c")))
                    for cb in range(NH):
                        aconv.append((2, cb, dwo[cb]))
                wsrc = {"g": (dwg4, s_gs), "u": (dwu4, s_us)}

                def loadconv(k):
                    ib, wi = k // 2, k % 2
                    raw = wsrc["g" if wi == 0 else "u"][0]
                    WB, BWB = s_wbf[k % 10], b_wbf[k % 10]
                    WR, BWR = s_wraw[k % 2], b_wraw[k % 2]
                    em.dma("dS", WR[:, :], raw[ib], writes=[BWR])
                    em.op("act", lambda a: a.activation(
                        WB[:, :], WR[:, :], AF.Copy, bias=-128.0),
                        reads=[BWR], writes=[BWB])

                def auxconv(ib):
                    sl = ib % 2
                    if ib < len(aconv):
                        wi2, cb, src_ap = aconv[ib]
                        if wi2 < 2:
                            em.dma("dS",
                                   s_af[sl][:, :].rearrange(
                                       "p (d c) -> p d c", d=NH),
                                   src_ap, writes=[b_af[sl]])
                        else:
                            em.dma("dS", s_af[sl][:, :], src_ap,
                                   writes=[b_af[sl]])
                        em.op("act", lambda a, sl=sl: a.activation(
                            s_ab[sl][:, :], s_af[sl][:, :], AF.Copy),
                            reads=[b_af[sl]], writes=[b_ab[sl]])
                        em.dma("dP", dwbfa[wi2, cb], s_ab[sl][:, :],
                               reads=[b_ab[sl]])

                WARM = 5
                for kw in range(2 * WARM):
                    loadconv(kw)

                def step_evac(ib, wi, o, wd, pap, pbuf):
                    scale2 = wsrc["g" if wi == 0 else "u"][1]
                    SL, BSL = s_silu[ib % 5], b_silu[ib % 5]
                    if wi == 0:
                        em.op("act", lambda a: a.activation(
                            SL[:, o:o + wd], pap, AF.Silu,
                            scale=scale2[:, ib:ib + 1]),
                            reads=[pbuf, b_c], writes=[BSL])
                    else:
                        HT, BHT = s_h[ib % 5], b_h[ib % 5]
                        em.op("dve", lambda v: v.scalar_tensor_tensor(
                            HT[:, o:o + wd], pap, scale2[:, ib:ib + 1],
                            SL[:, o:o + wd], ALU.mult, ALU.mult),
                            reads=[pbuf, BSL, b_c], writes=[BHT])

                lnd = [0]
                for (o, wd) in tgs:
                    drive(ln_gen, lnd, (o + wd + 127) // 128 + 4)
                    for kw in range(2 * WARM):
                        ibw, wiw = kw // 2, kw % 2
                        WB, BWB = s_wbf[kw % 10], b_wbf[kw % 10]
                        if o < 2048:
                            pap, pbuf = pG[:, o:o + wd], b_pG[o // 512]
                        else:
                            pap, pbuf = pG2[wiw][:, 0:wd], b_pG2[wiw]
                        mms = [(pap, WB[:, hc * 128:(hc + 1) * 128],
                                xnT[:, hc * TT_loc + o: hc * TT_loc + o + wd])
                               for hc in range(NH)]
                        em.mm_plan([(pbuf, mms,
                                     bxnT[o // 128:(o + wd + 127) // 128])],
                                   reads=[BWB])
                        step_evac(ibw, wiw, o, wd, pap, pbuf)
                for ibw in range(WARM):
                    HT, BHT = s_h[ibw % 5], b_h[ibw % 5]
                    em.dma("dV",
                           dh4[0:TT_loc // 128, ibw, :, :]
                           .rearrange("s p t -> p s t"),
                           HT[:, :].rearrange("p (s t) -> p s t", t=128),
                           reads=[BHT])
                drain(ln_gen)
                if 2 * WARM < 2 * NI:
                    loadconv(2 * WARM)

                for ib in range(WARM, NI):
                    for wi, w in enumerate(("g", "u")):
                        k = ib * 2 + wi
                        WB, BWB = s_wbf[k % 10], b_wbf[k % 10]
                        if 2 * WARM + 1 <= k + 1 < 2 * NI:
                            loadconv(k + 1)
                        def pslice(o, wdth):
                            if o < 2048:
                                return pG[:, o:o + wdth], b_pG[o // 512]
                            return pG2[wi][:, 0:wdth], b_pG2[wi]
                        plan = []
                        for (o, wd) in tgs:
                            pap, pbuf = pslice(o, wd)
                            mms = [(pap, WB[:, hc * 128:(hc + 1) * 128],
                                    xnT[:, hc * TT_loc + o: hc * TT_loc + o + wd])
                                   for hc in range(NH)]
                            plan.append((pbuf, mms,
                                         bxnT[o // 128:(o + wd + 127) // 128]))
                        em.mm_plan(plan, reads=[BWB])
                        if w == "g":
                            for (o, wd) in tgs:
                                pap, pbuf = pslice(o, wd)
                                step_evac(ib, 0, o, wd, pap, pbuf)
                        else:
                            for (o, wd) in tgs:
                                pap, pbuf = pslice(o, wd)
                                step_evac(ib, 1, o, wd, pap, pbuf)
                            HT, BHT = s_h[ib % 5], b_h[ib % 5]
                            em.dma("dV",
                                   dh4[0:TT_loc // 128, ib, :, :]
                                   .rearrange("s p t -> p s t"),
                                   HT[:, :].rearrange("p (s t) -> p s t", t=128),
                                   reads=[BHT])
                            if first_call:
                                auxconv(ib - WARM)
            em.barrier()

        def ffn_down(al, TT_loc, first_call, res_srcs, dst, scale_half):
            nT_loc = TT_loc // 128
            HH = H // 2
            with ExitStack() as ps:
                a2 = ps.enter_context
                s_wd = a2(nc.sbuf_tensor(uname("wdsb"), [128, NI * HH], BF16))
                b_wd = Buf(uname("wd"))
                s_hts = [a2(nc.sbuf_tensor(uname("hts"), [128, 2048], BF16))
                         for _ in range(3)]
                b_hts = [Buf(uname("hts")) for _ in range(3)]
                s_dsb = a2(nc.sbuf_tensor(uname("dsbl"), [128, H], F32))
                b_dsb = Buf(uname("dsb"))
                s_xr = [a2(nc.sbuf_tensor(uname("xr"), [128, HH], F32))
                        for _ in range(2)]
                b_xr = [Buf(uname("xr")) for _ in range(2)]
                s_t1 = [a2(nc.sbuf_tensor(uname("t1"), [128, HH], F32))
                        for _ in range(2)]
                b_t1 = [Buf(uname("t1")) for _ in range(2)]
                s_xo = [a2(nc.sbuf_tensor(uname("xo"), [128, HH], F32))
                        for _ in range(2)]
                b_xo = [Buf(uname("xo")) for _ in range(2)]
                pD = [a2(nc.psum_tensor(uname("pD"), [128, 1024], F32))
                      for _ in range(2)]
                b_pD = [Buf(uname("pD"), excl=True) for _ in range(2)]
                em.dma("dS", s_dsb[:, :], ddsb, writes=[b_dsb])

                rows = []
                for (src, row0, ntiles) in res_srcs:
                    for kk in range(ntiles):
                        rows.append((src, row0 + kk * 128))

                s_wdu8 = [a2(nc.sbuf_tensor(uname("wdu8"), [128, 8 * HH],
                                             U8)) for _ in range(2)]
                b_wdu8 = [Buf(uname("wdu8")) for _ in range(2)]
                b_wdc = [Buf(uname("wdc")) for _ in range(8)]
                for half in range(2):
                    for c8 in range(8):
                        em.dma("dS",
                               s_wdu8[c8 % 2][:, :]
                               .rearrange("p (i c) -> p i c", i=8),
                               dwd4[half, c8 * 8:(c8 + 1) * 8]
                               .rearrange("i p c -> p i c"),
                               writes=[b_wdu8[c8 % 2]])
                        em.op("act", lambda a, c8=c8: a.activation(
                            s_wd[:, c8 * 8 * HH:(c8 + 1) * 8 * HH],
                            s_wdu8[c8 % 2][:, :], AF.Copy, bias=-128.0),
                            reads=[b_wdu8[c8 % 2]], writes=[b_wdc[c8]])

                    for ts in range(nT_loc):
                        sl = ts % 2
                        for icb in range(4):
                            HB = s_hts[(ts * 4 + icb) % 3]
                            BHB = b_hts[(ts * 4 + icb) % 3]
                            em.dma("dA",
                                   HB[:, :].rearrange("p (i t) -> p i t", i=16),
                                   dh4[ts, icb * 16:(icb + 1) * 16, :, :]
                                   .rearrange("i p t -> p i t"), writes=[BHB])
                            em.dep_wait("pe", [BHB, b_wdc[2 * icb],
                                         b_wdc[2 * icb + 1]],
                                        [b_pD[sl]] if icb == 0 else [])
                            last_ins = None
                            for tg in range(2):
                                for il in range(16):
                                    ic = icb * 16 + il
                                    last_ins = nc.tensor.matmul(
                                        pD[sl][:, tg * 512:(tg + 1) * 512],
                                        HB[:, il * 128:(il + 1) * 128],
                                        s_wd[:, ic * HH + tg * 512:
                                             ic * HH + (tg + 1) * 512],
                                        start=(icb == 0 and il == 0),
                                        stop=(icb == 3 and il == 15))
                            em._finish(last_ins, "pe", 1,
                                       [BHB, b_wdc[2 * icb],
                                        b_wdc[2 * icb + 1]],
                                       [b_pD[sl]] if icb == 3 else [])
                        src, row0 = rows[ts]
                        em.dma("dA", s_xr[sl][:, :],
                               src[row0:row0 + 128, half * HH:(half + 1) * HH],
                               writes=[b_xr[sl]])
                        em.op("dve", lambda v, sl=sl, half=half: v.tensor_tensor(
                            s_t1[sl][:, :], pD[sl][:, :],
                            s_dsb[:, half * HH:(half + 1) * HH], ALU.mult),
                            reads=[b_pD[sl], b_dsb], writes=[b_t1[sl]])
                        em.op("dve", lambda v, sl=sl: v.scalar_tensor_tensor(
                            s_xo[sl][:, :], s_t1[sl][:, :], scale_half,
                            s_xr[sl][:, :], ALU.mult, ALU.add),
                            reads=[b_t1[sl], b_xr[sl]], writes=[b_xo[sl]])
                        em.dma("dV",
                               dst[ts * 128:(ts + 1) * 128,
                                   half * HH:(half + 1) * HH],
                               s_xo[sl][:, :], reads=[b_xo[sl]])
            em.barrier()

        with ExitStack() as ph:
            alP = ph.enter_context
            xnT = alP(nc.sbuf_tensor("xn1T", [128, NH * TT], BF16))
            bxnT = [Buf(f"xn1T{i}") for i in range(nT)]
            ln_srcs = ([(dx_pre, 0, nPre)] if nPre else []) + [(dx_own, 0, nTo)]
            g_ln = ln_transpose(alP, ln_srcs, xnT, bxnT, TT)
            if stop <= 1:
                drain(g_ln)
                em.drain_dmas()
                return nc
            ffn_gu(alP, xnT, bxnT, TT, True, ln_gen=g_ln)
        em.barrier()
        if stop <= 2:
            em.drain_dmas()
            return nc
        ffn_down(None, TT, True,
                 ([(dx_pre, 0, nPre)] if nPre else []) + [(dx_own, 0, nTo)],
                 dx1, 0.5)
        if stop <= 3:
            em.drain_dmas()
            return nc

        with ExitStack() as ph:
            alA = ph.enter_context
            s_o = alA(nc.sbuf_tensor("s_o", [128, NH * T_OWN], BF16))
            b_o = Buf("o")
            s_rb = alA(nc.sbuf_tensor("s_rb", [128, T_OWN], F32))
            b_rr = Buf("rr")
            xph = ph.enter_context(ExitStack())
            alX = xph.enter_context
            xnT = alX(nc.sbuf_tensor("xn2T", [128, NH * TT], BF16))
            bxnT = [Buf(f"xn2T{i}") for i in range(nT)]
            s_kf = alX(nc.sbuf_tensor("s_kf", [128, TT], F32))
            b_kf = Buf("kf")
            s_qf = alX(nc.sbuf_tensor("s_qf", [128, T_OWN], BF16))
            b_qf = Buf("qf")
            qst = ExitStack()
            s_qraw = qst.enter_context(
                nc.sbuf_tensor("s_qraw", [128, 1024], BF16))
            b_qraw = [Buf(uname("qraw")) for _ in range(2)]

            with ExitStack() as ps:
                a2 = ps.enter_context
                g_ln = ln_transpose(a2, [(dx1, 0, nT)], xnT, bxnT, TT,
                                    tok_dst=dxn2, nxn=2)
                if stop <= 4:
                    drain(g_ln)
                    em.drain_dmas()
                    return nc
                s_wqk = a2(nc.sbuf_tensor("s_wqk", [128, 2 * H], BF16))
                b_wqk = Buf("wqk")
                pK = a2(nc.psum_tensor("pKq", [128, 1024], F32))
                b_pK = [Buf(uname("pKq"), excl=True) for i in range(2)]
                pW = a2(nc.psum_tensor("pWg", [128, 1024], F32))
                b_pW = [Buf(f"pWg{i}", excl=True) for i in range(2)]
                pS = [a2(nc.psum_tensor(uname("pSm2"), [128, 128], BF16))
                      for _ in range(2)]
                b_pS = [Buf(uname("pSm2"), excl=True) for _ in range(2)]
                s_qe = a2(nc.sbuf_tensor("s_qe", [128, 260], F32))
                b_qe = [Buf(uname("qe")) for _ in range(2)]
                s_qb = a2(nc.sbuf_tensor("s_qb", [128, 256], BF16))
                b_qb = [Buf(uname("qb")) for _ in range(2)]
                s_wigb = [a2(nc.sbuf_tensor(uname("wigb"), [128, H], BF16))
                          for _ in range(2)]
                b_wigb = [Buf(uname("wigb")) for _ in range(2)]
                s_gwo = [a2(nc.sbuf_tensor(uname("gwo"), [128, 512], BF16))
                         for _ in range(2)]
                b_gwo = [Buf(uname("gwo")) for _ in range(2)]

                for wi, wsrc_ap in enumerate((dwq, dwk)):
                    w3 = wsrc_ap.rearrange("(h p) d -> p h d", p=128)
                    for part in range(2):
                        em.dma("dP",
                               s_wqk[:, (2 * wi + part) * (H // 2):
                                     (2 * wi + part + 1) * (H // 2)]
                               .rearrange("p (h d) -> p h d", h=NH // 2),
                               w3[:, part * (NH // 2):(part + 1) * (NH // 2), :],
                               writes=[b_wqk])

                lnd = [0]
                pKu = [0]

                def proj(stat_off, o_tok, wd, tiles):
                    sl = pKu[0] % 2
                    pKu[0] += 1
                    pap = pK[:, sl * 512: sl * 512 + wd]
                    mms = [(pap, s_wqk[:, stat_off + hc * 128:
                                       stat_off + (hc + 1) * 128],
                            xnT[:, hc * TT + o_tok: hc * TT + o_tok + wd])
                           for hc in range(NH)]
                    em.mm_plan([(b_pK[sl], mms, tiles)], reads=[b_wqk])
                    return pap, b_pK[sl]

                def softmax_chunk(c):
                    oo = c * 512
                    qsl = c % 2
                    bqr = b_qraw[qsl]
                    pap, pbuf = proj(0, T_PRE + oo, 512,
                                     bxnT[(T_PRE + oo) // 128:
                                          (T_PRE + oo + 512) // 128])
                    em.op("dve", lambda v, qsl=qsl, pap=pap: v.tensor_copy(
                        s_qraw[:, qsl * 512:qsl * 512 + 512], pap),
                        reads=[pbuf], writes=[bqr])
                    for tt in range(4 * c, 4 * c + 4):
                        sl2 = tt % 2
                        qo = qsl * 512 + (tt - 4 * c) * 128
                        em.op("pe", lambda t, qo=qo, sl2=sl2: t.transpose(
                            pS[sl2][:, :], s_qraw[:, qo:qo + 128],
                            s_ident[:, :]), reads=[bqr, b_c],
                            writes=[b_pS[sl2]])
                        QT = pS[sl2][:, :]
                        bqt = b_pS[sl2]
                        QE = s_qe[:, sl2 * 130:(sl2 + 1) * 130]
                        bqe, bqb = b_qe[sl2], b_qb[sl2]
                        QB = s_qb[:, sl2 * 128:(sl2 + 1) * 128]
                        em.op("dve", lambda v, QT=QT, QE=QE: v.reduce_max(
                            QE[:, 128:129], QT, AX.X, negate=True),
                            reads=[bqt], writes=[bqe])
                        em.op("act", lambda a, QT=QT, QE=QE: a.activation(
                            QE[:, 0:128], QT, AF.Exp,
                            bias=QE[:, 128:129], accum_out=QE[:, 129:130]),
                            reads=[bqt, bqe], writes=[bqe])
                        em.op("dve", lambda v, QE=QE: v.reciprocal(
                            QE[:, 128:129], QE[:, 129:130]),
                            reads=[bqe], writes=[bqe])
                        em.op("dve", lambda v, QE=QE, QB=QB: v.tensor_scalar(
                            QB, QE[:, 0:128], QE[:, 128:129], None,
                            ALU.mult), reads=[bqe], writes=[bqb])
                        em.op("pe", lambda t, QB=QB, sl2=sl2: t.transpose(
                            pS[sl2][:, :], QB, s_ident[:, :]),
                            reads=[bqb, b_c], writes=[b_pS[sl2]])
                        em.op("dve", lambda v, tt=tt, sl2=sl2: v.tensor_copy(
                            s_qf[:, tt * 128:(tt + 1) * 128], pS[sl2][:, :]),
                            reads=[b_pS[sl2]], writes=[b_qf])

                wig_pp = [0]

                def wig_pass(c0, nck):
                    for cb in range(NH):
                        sl = cb % 2
                        em.dma("dS", s_wigb[sl][:, :], dwbfa[1, cb],
                               writes=[b_wigb[sl]])
                        for c in range(c0, c0 + nck):
                            oo = c * 512
                            gi = wig_pp[0] % 2
                            wig_pp[0] += 1
                            pap = pW[:, gi * 512:gi * 512 + 512]
                            pbuf = b_pW[gi]
                            mms = [(pap,
                                    s_wigb[sl][:, hc * 128:(hc + 1) * 128],
                                    xnT[:, hc * TT + T_PRE + oo:
                                        hc * TT + T_PRE + oo + 512])
                                   for hc in range(NH)]
                            em.mm_plan([(pbuf, mms,
                                         bxnT[(T_PRE + oo) // 128:
                                              (T_PRE + oo + 512) // 128])],
                                       reads=[b_wigb[sl]])
                            em.op("act", lambda a, pap=pap, sl=sl:
                                  a.activation(s_gwo[sl][:, :], pap, AF.Copy),
                                  reads=[pbuf], writes=[b_gwo[sl]])
                            em.dma("dV", dgw[c, cb], s_gwo[sl][:, :],
                                   reads=[b_gwo[sl]])

                nck_all = nTo // 4
                done_c = 0
                groups = tgroups(TT)
                for ng, (o, wd) in enumerate(groups):
                    drive(g_ln, lnd, (o + wd + 127) // 128 + 4)
                    pap, pbuf = proj(H, o, wd, bxnT[o // 128:
                                                    (o + wd + 127) // 128])
                    em.op("act", lambda a, o=o, wd=wd, pap=pap: a.activation(
                        s_kf[:, o:o + wd], pap, AF.Sigmoid),
                        reads=[pbuf], writes=[b_kf])
                    avail_c = min(max(0, o + wd - T_PRE) // 512, nck_all)
                    while done_c < avail_c:
                        softmax_chunk(done_c)
                        wig_pass(done_c, 1)
                        done_c += 1
                drain(g_ln)

            em.barrier()
            qst.close()
            if stop <= 5:
                em.drain_dmas()
                return nc
            with ExitStack() as ps:
                a2 = ps.enter_context
                s_S = [a2(nc.sbuf_tensor(uname("s_S"), [128, H], F32))
                       for _ in range(2)]
                b_S = [Buf(uname("S")) for _ in range(2)]
                s_Sb = [a2(nc.sbuf_tensor(uname("s_Sb"), [128, H], BF16))
                        for _ in range(2)]
                b_Sb = [Buf(uname("Sb")) for _ in range(2)]
                s_pr = [a2(nc.sbuf_tensor(uname("s_pr"), [128, 132], F32))
                        for _ in range(3)]
                b_pr = [Buf(uname("pr")) for _ in range(3)]
                s_pe = [a2(nc.sbuf_tensor(uname("s_pe"), [128, 512], BF16))
                        for _ in range(2)]
                b_pe = [Buf(uname("pe_")) for _ in range(2)]
                s_pb = [a2(nc.sbuf_tensor(uname("s_pb"), [128, 1024], BF16))
                        for _ in range(3)]
                b_pb = [Buf(uname("pb")) for _ in range(3)]
                b_pbm = [Buf(uname("pbm")) for _ in range(3)]
                s_v = [a2(nc.sbuf_tensor(uname("s_v"), [128, H], BF16))
                       for _ in range(3)]
                b_v = [Buf(uname("v")) for _ in range(3)]
                s_rsq = a2(nc.sbuf_tensor("s_rsq", [128, H], BF16))
                b_rsq = Buf("rsq")
                pO = a2(nc.psum_tensor("pOs", [128, 2048], F32))
                b_pO = [Buf(f"pO{i}", excl=True) for i in range(2)]
                pM = a2(nc.psum_tensor("pMs", [128, 1024], F32))
                b_pM = Buf("pM", excl=True)
                pAT = a2(nc.psum_tensor("pATs", [128, 128], F32))
                b_pAT = Buf("pAT", excl=True)
                pTx = a2(nc.psum_tensor("pTx", [128, 1024], BF16))
                b_pTx = Buf("pTx", excl=True)

                em.op("dve", lambda v: v.memset(s_S[0][:, :], 0.0), writes=[b_S[0]])
                for sbi in range(2):
                    em.op("dve", lambda v, sbi=sbi: v.memset(
                        s_Sb[sbi][:, :], 0.0), writes=[b_Sb[sbi]])
                o3 = s_o[:, :].rearrange("p (cx t) -> p cx t", t=T_OWN)

                def prep_a(p):
                    pi = p % 3
                    own = p >= nPre
                    PR, BPR = s_pr[pi], b_pr[pi]
                    PEX, BPE = s_pe[p % 2], b_pe[p % 2]
                    PB, BPB = s_pb[pi], b_pb[pi]
                    em.op("act", lambda a: a.activation(
                        PR[:, 0:128], s_kf[:, p * 128:(p + 1) * 128], AF.Ln,
                        scale=-1.0, bias=1.0), reads=[b_kf], writes=[BPR])
                    em.op("dve", lambda v: v.tensor_tensor_scan(
                        PR[:, 0:128], PR[:, 0:128], PR[:, 0:128], 0.0,
                        ALU.add, ALU.bypass), reads=[BPR], writes=[BPR])
                    em.op("dve", lambda v: v.tensor_scalar(
                        PR[:, 128:129], PR[:, 63:64], -1.0, None, ALU.mult),
                        reads=[BPR], writes=[BPR])
                    em.op("act", lambda a: a.activation(
                        PR[:, 130:131], PR[:, 127:128], AF.Exp),
                        reads=[BPR], writes=[BPR])
                    em.op("act", lambda a: a.activation(
                        PEX[:, 256:384], PR[:, 0:128], AF.Exp, scale=-1.0,
                        bias=PR[:, 127:128]), reads=[BPR], writes=[BPE])
                    if own:
                        po = p - nPre
                        em.op("act", lambda a: a.activation(
                            PEX[:, 0:128], PR[:, 0:128], AF.Exp,
                            bias=PR[:, 128:129]), reads=[BPR], writes=[BPE])
                        em.op("act", lambda a: a.activation(
                            PEX[:, 128:256], PR[:, 0:128], AF.Exp),
                            reads=[BPR], writes=[BPE])
                        em.op("act", lambda a: a.activation(
                            PEX[:, 384:512], PR[:, 0:128], AF.Exp, scale=-1.0,
                            bias=PR[:, 63:64]), reads=[BPR], writes=[BPE])
                        em.op("dve", lambda v, po=po: v.tensor_tensor(
                            PB[:, 0:128], s_qf[:, po * 128:(po + 1) * 128],
                            PEX[:, 0:128], ALU.mult), reads=[b_qf, BPE],
                            writes=[BPB])
                        em.op("dve", lambda v, po=po: v.tensor_tensor(
                            PB[:, 128:256], s_qf[:, po * 128:(po + 1) * 128],
                            PEX[:, 128:256], ALU.mult), reads=[b_qf, BPE],
                            writes=[BPB])
                        em.op("dve", lambda v: v.tensor_scalar(
                            PB[:, 896:1024], PB[:, 128:256],
                            s_pr[(p - 1) % 3][:, 130:131], None, ALU.mult),
                            reads=[BPB, b_pr[(p - 1) % 3]], writes=[BPB])
                        em.op("dve", lambda v: v.tensor_tensor(
                            PB[:, 256:384], s_kf[:, p * 128:(p + 1) * 128],
                            PEX[:, 384:512], ALU.mult), reads=[b_kf, BPE],
                            writes=[BPB])

                def prep_b1(p):
                    pi = p % 3
                    PEX, BPE = s_pe[p % 2], b_pe[p % 2]
                    PB, BPB = s_pb[pi], b_pb[pi]
                    if p < nT - 1:
                        em.op("dve", lambda v: v.tensor_tensor(
                            PB[:, 384:512], s_kf[:, p * 128:(p + 1) * 128],
                            PEX[:, 256:384], ALU.mult),
                            reads=[b_kf, BPE], writes=[BPB])
                        em.op("pe", lambda t: t.transpose(
                            pTx[:, 0:128], PB[:, 384:512], s_ident[:, :]),
                            reads=[BPB, b_c], writes=[b_pTx])
                        em.op("act", lambda a: a.activation(
                            PB[:, 512:640], pTx[:, 0:128], AF.Copy),
                            reads=[b_pTx], writes=[BPB])

                def prep_b2(p):
                    pi = p % 3
                    V, BV = s_v[pi], b_v[pi]
                    em.dma("dS", V[:, :], dxn2[p], writes=[BV])

                def math_a(p):
                    pi = p % 3
                    own = p >= nPre
                    if not own:
                        return
                    PB, BPB = s_pb[pi], b_pb[pi]
                    PBv, BPBv = s_pb[(p - 1) % 3], b_pb[(p - 1) % 3]
                    if True:
                        em.op("dve", lambda v: v.memset(
                            PB[64:128, 640:704], 0.0), writes=[b_pbm[pi]])
                        em.op("pe", lambda t: t.matmul(
                            pM[:, 0:128], PBv[:, 384:512], PB[:, 128:256],
                            start=True, stop=True), reads=[BPBv, BPB],
                            writes=[b_pM])
                        em.op("dve", lambda v: v.tensor_copy(
                            PB[:, 768:896], pM[:, 0:128]),
                            reads=[b_pM], writes=[BPB])
                        em.op("pe", lambda t: t.matmul(
                            pAT[0:64, :], PB[:, 256:320], PB[:, 0:128],
                            start=True, stop=True), reads=[BPB], writes=[b_pAT])
                        em.op("pe", lambda t: t.matmul(
                            pAT[64:128, 64:128], PB[:, 320:384], PB[:, 64:128],
                            start=True, stop=True), reads=[BPB], writes=[b_pAT])
                        em.op("dve", lambda v: v.tensor_tensor(
                            PB[0:64, 640:768], pAT[0:64, :], s_umask[0:64, :],
                            ALU.min), reads=[b_pAT, b_c], writes=[b_pbm[pi]])
                        em.op("dve", lambda v: v.tensor_tensor(
                            PB[64:128, 704:768], pAT[64:128, 64:128],
                            s_umask[64:128, 64:128], ALU.min),
                            reads=[b_pAT, b_c], writes=[b_pbm[pi]])

                def math_b(p):
                    pi = p % 3
                    own = p >= nPre
                    po = p - nPre
                    PR, BPR = s_pr[pi], b_pr[pi]
                    PB, BPB = s_pb[pi], b_pb[pi]
                    V, BV = s_v[pi], b_v[pi]
                    S_in, bS_in = s_S[p % 2], b_S[p % 2]
                    S_out, bS_out = s_S[(p + 1) % 2], b_S[(p + 1) % 2]
                    SB, bSB = s_Sb[p % 2], b_Sb[p % 2]
                    if own:
                        PBv, BPBv = s_pb[(p - 1) % 3], b_pb[(p - 1) % 3]
                        Vv, BVv = s_v[(p - 1) % 3], b_v[(p - 1) % 3]
                        for hf in range(2):
                            em.dep_wait("pe", [BPB, BV, BVv, bSB], [b_pO[hf]])
                            for dd in range(NH // 2):
                                dvc = hf * (NH // 2) + dd
                                nc.tensor.matmul(
                                    pO[:, dvc * 128:(dvc + 1) * 128],
                                    Vv[:, dvc * 128:(dvc + 1) * 128],
                                    PB[:, 768:896], start=True, stop=False)
                                nc.tensor.matmul(
                                    pO[:, dvc * 128:(dvc + 1) * 128],
                                    SB[:, dvc * 128:(dvc + 1) * 128],
                                    PB[:, 896:1024], start=False, stop=False)
                                em.dep_wait("pe", [b_pbm[pi]], [])
                                i2 = nc.tensor.matmul(
                                    pO[:, dvc * 128:(dvc + 1) * 128],
                                    V[:, dvc * 128:(dvc + 1) * 128],
                                    PB[:, 640:768], start=False, stop=True)
                                if dd == NH // 2 - 1:
                                    em._finish(i2, "pe", 1,
                                               [BPB, BV, BVv, bSB,
                                                b_pbm[pi]],
                                               [b_pO[hf]])
                            if hf == 0:
                                em.op("act", lambda a, po=po: a.activation(
                                    o3[:, 0:8, po * 128:(po + 1) * 128],
                                    pO[:, 0:1024]
                                    .rearrange("p (cx t) -> p cx t", t=128),
                                    AF.Copy), reads=[b_pO[0]], writes=[b_o])
                            else:
                                em.op("dve", lambda v, po=po: v.tensor_copy(
                                    o3[:, 8:16, po * 128:(po + 1) * 128],
                                    pO[:, 1024:2048]
                                    .rearrange("p (cx t) -> p cx t", t=128)),
                                    reads=[b_pO[1]], writes=[b_o])
                def math_m(p):
                    pi = p % 3
                    PR, BPR = s_pr[pi], b_pr[pi]
                    PB, BPB = s_pb[pi], b_pb[pi]
                    V, BV = s_v[pi], b_v[pi]
                    S_in, bS_in = s_S[p % 2], b_S[p % 2]
                    S_out, bS_out = s_S[(p + 1) % 2], b_S[(p + 1) % 2]
                    SB, bSB = s_Sb[p % 2], b_Sb[p % 2]
                    if p >= nT - 2:
                        return
                    for mh in range(2):
                        pMap = pM[:, 0:1024] if mh == 0 else pO[:, 0:1024]
                        bpMm = b_pM if mh == 0 else b_pO[0]
                        em.dep_wait("pe", [BPB, BV], [bpMm])
                        for tg in range(2):
                            ins = nc.tensor.matmul(
                                pMap[:, tg * 512:(tg + 1) * 512],
                                PB[:, 512:640],
                                V[:, mh * 1024 + tg * 512:
                                  mh * 1024 + (tg + 1) * 512],
                                start=True, stop=True)
                            if tg == 1:
                                em._finish(ins, "pe", 1, [BPB, BV], [bpMm])
                        em.op("dve", lambda v, mh=mh, pMap=pMap:
                              v.scalar_tensor_tensor(
                            S_out[:, mh * 1024:(mh + 1) * 1024],
                            S_in[:, mh * 1024:(mh + 1) * 1024],
                            PR[:, 130:131], pMap, ALU.mult, ALU.add),
                            reads=[bpMm, BPR, bS_in], writes=[bS_out])
                        em.op("dve", lambda v, mh=mh: v.tensor_copy(
                            SB[:, mh * 1024:(mh + 1) * 1024],
                            S_out[:, mh * 1024:(mh + 1) * 1024]),
                            reads=[bS_out], writes=[bSB])

                def rstat(po):
                    em.op("act", lambda a, po=po: a.activation(
                        s_rsq[:, :], o3[:, :, po * 128:(po + 1) * 128],
                        AF.Square), reads=[b_o], writes=[b_rsq])
                    em.dep_wait("pe", [b_rsq, b_c], [b_pAT])
                    for dvc in range(NH):
                        ins = nc.tensor.matmul(
                            pAT[0:1, 0:128], s_ones1[:, :],
                            s_rsq[:, dvc * 128:(dvc + 1) * 128],
                            start=(dvc == 0), stop=(dvc == NH - 1))
                        if dvc == NH - 1:
                            em._finish(ins, "pe", 1, [b_rsq, b_c], [b_pAT])
                    em.op("act", lambda a, po=po: a.activation(
                        s_rb[0:1, po * 128:(po + 1) * 128], pAT[0:1, 0:128],
                        AF.Ln, scale=s_cst[0:1, 1:2], bias=s_cst[0:1, 0:1]),
                        reads=[b_pAT, b_c], writes=[b_rr])
                    em.op("act", lambda a, po=po: a.activation(
                        s_rb[0:1, po * 128:(po + 1) * 128],
                        s_rb[0:1, po * 128:(po + 1) * 128], AF.Exp, scale=-0.5),
                        reads=[b_rr], writes=[b_rr])

                prep_a(0)
                prep_b1(0)
                prep_b2(0)
                prep_a(1)
                prep_b1(1)
                prep_b2(1)
                for p in range(nT):
                    math_a(p)
                    if p + 2 < nT:
                        prep_a(p + 2)
                    math_b(p)
                    if p + 2 < nT:
                        prep_b1(p + 2)
                    math_m(p)
                    if p >= nPre:
                        rstat(p - nPre)
                    if p + 2 < nT:
                        prep_b2(p + 2)

            xph.close()
            em.barrier()
            if stop <= 6:
                em.drain_dmas()
                return nc
            s_wo = alA(nc.sbuf_tensor("s_wo", [128, NH * H], BF16))
            b_wo = Buf("wo")
            for cc in range(NH):
                em.dma("dV", s_wo[:, cc * H:(cc + 1) * H], dwbfa[2, cc],
                       writes=[b_wo])
            with ExitStack() as ps:
                a2 = ps.enter_context
                b_rb = b_rr
                pGo = a2(nc.psum_tensor("pGo", [128, 2048], F32))
                b_pGo = [Buf(f"pGo{i}", excl=True) for i in range(4)]
                for (o, wd) in tgroups(T_OWN):
                    em.op("pe", lambda t, o=o, wd=wd: t.matmul(
                        pGo[:, 0:wd], s_onesr[0:1, :], s_rb[0:1, o:o + wd],
                        start=True, stop=True), reads=[b_rr, b_c],
                        writes=[b_pGo[0]])
                    em.op("dve", lambda v, o=o, wd=wd: v.tensor_copy(
                        s_rb[:, o:o + wd], pGo[:, 0:wd]),
                        reads=[b_pGo[0]], writes=[b_rb])

                s_wb2 = [a2(nc.sbuf_tensor(uname("s_wb2"), [128, 2048], BF16))
                         for _ in range(2)]
                b_wb2 = [Buf(uname("wb2")) for _ in range(2)]
                s_gwi = [a2(nc.sbuf_tensor(uname("gwi"), [128, 1024], BF16))
                         for _ in range(2)]
                b_gwi = [Buf(uname("gwi")) for _ in range(2)]
                s_gof = a2(nc.sbuf_tensor("s_gof", [128, 1024], F32))
                b_gof = Buf("gof")
                s_sg = a2(nc.sbuf_tensor("s_sg", [128, 1024], BF16))
                b_sg = Buf("sg")
                s_yb = a2(nc.sbuf_tensor("s_yb", [128, NH * min(1024, T_OWN)],
                                          BF16))
                b_yb = Buf("yb")
                s_xr2 = a2(nc.sbuf_tensor("s_xr2", [128, H], F32))
                b_xr2 = Buf("xr2")
                s_xo2 = a2(nc.sbuf_tensor("s_xo2", [128, H], F32))
                b_xo2 = Buf("xo2")

                SW = min(1024, T_OWN)
                for c in range(T_OWN // SW):
                    for cb in range(NH):
                        slw = cb % 2
                        WB, BWB = s_wb2[slw], b_wb2[slw]
                        em.dma("dS", WB[:, :], dwbfa[0, cb], writes=[BWB])
                        for g2 in range(SW // 512):
                            em.dma("dA", s_gwi[slw][:, g2 * 512:
                                                    g2 * 512 + 512],
                                   dgw[(c * SW) // 512 + g2, cb],
                                   writes=[b_gwi[slw]])
                        g0 = (2 * cb) % 4
                        for g2 in range(SW // 512):
                            gi = g0 + g2
                            pap = pGo[:, gi * 512:(gi + 1) * 512]
                            mms = [(pap, WB[:, dvc * 128:(dvc + 1) * 128],
                                    s_o[:, dvc * T_OWN + c * SW + g2 * 512:
                                        dvc * T_OWN + c * SW + g2 * 512
                                        + 512])
                                   for dvc in range(NH)]
                            em.mm_plan([(b_pGo[gi], mms)], reads=[BWB, b_o])
                        gbk = [b_pGo[g0 + g2] for g2 in range(SW // 512)]
                        em.op("dve", lambda v, g0=g0, slw=slw:
                              v.tensor_tensor(
                                  s_gof[:, 0:SW],
                                  pGo[:, g0 * 512:g0 * 512 + SW],
                                  s_gwi[slw][:, 0:SW], ALU.add),
                              reads=gbk + [b_gwi[slw]],
                              writes=[b_gof])
                        em.op("act", lambda a: a.activation(
                            s_sg[:, 0:SW], s_gof[:, 0:SW], AF.Silu),
                            reads=[b_gof], writes=[b_sg])
                        em.op("dve", lambda v, cb=cb, c=c:
                              v.scalar_tensor_tensor(
                                  s_gof[:, 0:SW],
                                  s_o[:, cb * T_OWN + c * SW:
                                      cb * T_OWN + c * SW + SW],
                                  s_gnw[:, cb:cb + 1],
                                  s_rb[:, c * SW:c * SW + SW],
                                  ALU.mult, ALU.mult),
                              reads=[b_o, b_rb, b_c], writes=[b_gof])
                        em.op("dve", lambda v, cb=cb: v.tensor_tensor(
                            s_yb[:, cb * SW:(cb + 1) * SW], s_gof[:, 0:SW],
                            s_sg[:, 0:SW], ALU.mult),
                            reads=[b_gof, b_sg], writes=[b_yb])
                    for sub in range(SW // 128):
                        t0 = c * SW + sub * 128
                        em.dma("dA", s_xr2[:, :],
                               dx1[T_PRE + t0: T_PRE + t0 + 128, :],
                               writes=[b_xr2])
                        for half in range(2):
                            bk = [b_pGo[2 * half], b_pGo[2 * half + 1]]
                            em.dep_wait("pe", [b_yb, b_wo], bk)
                            for cc in range(NH):
                                for tg in range(2):
                                    ins = nc.tensor.matmul(
                                        pGo[:, half * 1024 + tg * 512:
                                            half * 1024 + (tg + 1) * 512],
                                        s_yb[:, cc * SW + sub * 128:
                                             cc * SW + sub * 128 + 128],
                                        s_wo[:, cc * H + half * 1024 + tg * 512:
                                             cc * H + half * 1024 +
                                             (tg + 1) * 512],
                                        start=(cc == 0), stop=(cc == NH - 1))
                                    if cc == NH - 1 and tg == 1:
                                        em._finish(ins, "pe", 1,
                                                   [b_yb, b_wo], bk)
                            em.op("dve", lambda v, half=half:
                                  v.tensor_tensor(
                                      s_xo2[:, half * 1024:(half + 1) * 1024],
                                      pGo[:, half * 1024:(half + 1) * 1024],
                                      s_xr2[:, half * 1024:(half + 1) * 1024],
                                      ALU.add),
                                  reads=bk + [b_xr2], writes=[b_xo2])
                        em.dma("dV", dx2[t0:t0 + 128, :],
                               s_xo2[:, :], reads=[b_xo2])
                em.barrier()

        em.barrier()
        if stop <= 7:
            em.drain_dmas()
            return nc
        if stop <= 8:
            em.drain_dmas()
            return nc
        with ExitStack() as ph:
            alP = ph.enter_context
            xnT = alP(nc.sbuf_tensor("xn3T", [128, NH * T_OWN], BF16))
            bxnT = [Buf(f"xn3T{i}") for i in range(nTo)]
            g_ln = ln_transpose(alP, [(dx2, 0, nTo)], xnT, bxnT, T_OWN)
            ffn_gu(alP, xnT, bxnT, T_OWN, False, ln_gen=g_ln)
        em.barrier()
        ffn_down(None, T_OWN, False, [(dx2, 0, nTo)], dout, 0.5)

        em.drain_dmas()
    return nc


def _layout_inputs(L0_attn_hidden_states, L0_attn_Wq, L0_attn_Wk, L0_attn_Wog,
                   L0_attn_Wig, L0_attn_Wo, L0_attn_g_norm_w, L0_ffn_gate_w,
                   L0_ffn_gate_scale, L0_ffn_up_w, L0_ffn_up_scale,
                   L0_ffn_down_w, L0_ffn_down_scale, T_OWN, T_PRE):
    x = np.asarray(L0_attn_hidden_states)
    B, L, _ = x.shape
    f32 = np.float32

    def cont(a, dt=f32):
        return np.ascontiguousarray(a, dtype=dt)

    wg4 = cont(np.transpose(np.asarray(L0_ffn_gate_w).reshape(NI, 128, NH, 128),
                            (0, 3, 2, 1)).reshape(NI, 128, H), np.uint8)
    wu4 = cont(np.transpose(np.asarray(L0_ffn_up_w).reshape(NI, 128, NH, 128),
                            (0, 3, 2, 1)).reshape(NI, 128, H), np.uint8)
    wd4 = cont(np.transpose(np.asarray(L0_ffn_down_w).reshape(NH, 128, NI, 128),
                            (2, 3, 0, 1)).reshape(NI, 128, H)
               .reshape(NI, 128, 2, H // 2).transpose(2, 0, 1, 3), np.uint8)
    gs2 = cont(np.asarray(L0_ffn_gate_scale).reshape(NI, 128).T)
    us2 = cont(np.asarray(L0_ffn_up_scale).reshape(NI, 128).T)
    dsb = cont(np.broadcast_to(np.asarray(L0_ffn_down_scale), (128, H)))
    gnw2 = cont(np.asarray(L0_attn_g_norm_w).reshape(NH, 128).T)
    wq2 = cont(np.asarray(L0_attn_Wq).T)
    wk2 = cont(np.asarray(L0_attn_Wk).T)
    wog4 = cont(np.transpose(np.asarray(L0_attn_Wog).reshape(NH, 128, NH, 128),
                             (0, 2, 3, 1)))
    wig4 = cont(np.transpose(np.asarray(L0_attn_Wig).reshape(NH, 128, NH, 128),
                             (0, 2, 3, 1)))
    wo3 = cont(np.asarray(L0_attn_Wo).T.reshape(NH, 128, H))
    ident = np.eye(128, dtype=f32).astype(ml_dtypes.bfloat16)
    jj, ii = np.meshgrid(np.arange(128), np.arange(128), indexing="ij")
    umask = np.where(jj <= ii, BIGMASK, 0.0).astype(f32)
    csts = np.broadcast_to(np.array([EPS, 1.0 / H], dtype=f32), (128, 2)).copy()

    shared = dict(wg4=wg4, wu4=wu4, wd4=wd4, gs2=gs2, us2=us2, dsb=dsb,
                  gnw2=gnw2, wq2=wq2, wk2=wk2, wog4=wog4, wig4=wig4, wo3=wo3,
                  ident=ident, umask=umask, csts=csts)
    in_maps = []
    halves = L // T_OWN
    n_cores = B * halves
    for c in range(n_cores):
        b, hh = c // halves, c % halves
        t0 = hh * T_OWN
        x_own = cont(x[b, t0:t0 + T_OWN, :])
        if t0 == 0:
            x_pre = np.zeros((T_PRE, H), dtype=f32)
        else:
            x_pre = cont(x[b, t0 - T_PRE:t0, :])
        in_maps.append(dict(shared, x_own=x_own, x_pre=x_pre))
    return in_maps, n_cores


def kernel(**inputs):
    T_OWN, T_PRE = 2048, 128
    in_maps, n_cores = _layout_inputs(**inputs, T_OWN=T_OWN, T_PRE=T_PRE)
    nc = build_program(T_OWN, T_PRE)
    res = run_bass_kernel_spmd(nc, in_maps, core_ids=list(range(n_cores)))
    x = np.asarray(inputs["L0_attn_hidden_states"])
    B, L, _ = x.shape
    halves = L // T_OWN
    out = np.empty((B, L, H), dtype=np.float32)
    for c in range(n_cores):
        b, hh = c // halves, c % halves
        out[b, hh * T_OWN:(hh + 1) * T_OWN, :] = res.results[c]["out"]
    return out
